# version 04
# baseline (speedup 1.0000x reference)
import numpy as np
import ml_dtypes
from contextlib import ExitStack

import concourse.bass as bass
import concourse.mybir as mybir
from concourse.bass_utils import run_bass_kernel_spmd

F32 = mybir.dt.float32
BF16 = mybir.dt.bfloat16
I8 = mybir.dt.int8
AF = mybir.ActivationFunctionType
ALU = mybir.AluOpType
AX = mybir.AxisListType
BF = ml_dtypes.bfloat16

S = 2048
DM = 2048
NCH = 16
TG = 512
NTG = S // TG
NTT = 16
NEG = -1.0e9

NH0, HD0, CBLK, RBLK = 16, 128, 128, 16
I0CH, Q0N = 64, 4
I1CH, Q1N = 44, 4
LN128 = float(np.log(128.0))

BARRIER_MODE = True
ENGINES = ("sync", "pool", "dve", "act", "pe")
BLOCK_ATTR = {"sync": "sync", "pool": "gpsimd", "dve": "vector", "act": "scalar", "pe": "tensor"}
SEMS = ("sdma", "pdma", "pe", "dve", "act", "pool")
DMA_SEM = {"sync": "sdma", "pool": "pdma"}



def tile_w(W):
    In, Out = W.shape
    nci = In // 128
    if Out < 128:
        return np.ascontiguousarray(
            W.reshape(nci, 128, 1, Out).transpose(2, 1, 0, 3).reshape(1, 128, nci * Out))
    nco = Out // 128
    return np.ascontiguousarray(
        W.reshape(nci, 128, nco, 128).transpose(2, 1, 0, 3).reshape(nco, 128, nci * 128))


def col_form(v):
    return np.ascontiguousarray(v.reshape(-1, 128).T)


def make_consts():
    c = {}
    c["idf"] = np.eye(128, dtype=np.float32)
    c["idb"] = np.eye(128).astype(BF)
    c["trif"] = np.triu(np.ones((128, 128))).astype(np.float32)
    c["l1maskb"] = np.triu(np.ones((128, 128))).astype(BF)
    c["ones_col_f"] = np.ones((128, 1), dtype=np.float32)
    c["ones_row_f"] = np.ones((1, 128), dtype=np.float32)
    c["onesblk"] = np.stack([np.arange(128) < 64, np.arange(128) >= 64], 1).astype(np.float32)
    inv = 10000.0 ** (-np.arange(32) / 32.0)
    t = np.arange(S)
    ang = t[None, :] * inv[:, None]
    c["l0cos"] = np.concatenate([np.cos(ang)] * 2, 0).astype(BF)
    c["l0ss"] = np.concatenate([-np.sin(ang), np.sin(ang)], 0).astype(BF)
    bend = np.arange(CBLK) * RBLK + (RBLK - 1)
    angc = bend[None, :] * inv[:, None]
    c["ckcos"] = np.concatenate([np.cos(angc)] * 2, 0).astype(BF)
    c["ckss"] = np.concatenate([-np.sin(angc), np.sin(angc)], 0).astype(BF)
    p = np.arange(64)
    invp = 10000.0 ** (-(p // 2) / 32.0)
    angl = t[None, :] * invp[:, None]
    sgn = np.where(p % 2 == 0, -1.0, 1.0)[:, None]
    c["l1cos"] = np.concatenate([np.cos(angl)] * 2, 0).astype(BF)
    c["l1ss"] = np.concatenate([sgn * np.sin(angl)] * 2, 0).astype(BF)
    i = np.arange(128)[:, None]
    j = np.arange(256)[None, :]
    band = (j >= i) & (j <= i + 128)
    c["mask_loc"] = np.where(band, 0.0, NEG).astype(np.float32)
    c["mask_loc0"] = np.where(band & (j >= 128), 0.0, NEG).astype(np.float32)
    jj = np.arange(264)[None, :]
    c["cmaskb"] = np.where(16 * jj <= 2033 + i, 0.0, NEG).astype(np.float32)
    sel = np.zeros((32, 2048), dtype=np.float32)
    for hh in range(32):
        sel[hh, hh * 64:(hh + 1) * 64] = 1.0
    c["selab"] = sel.astype(BF)
    return c


def prep_inputs(inp):
    g = {}
    f32 = np.float32
    g["w_q0"] = tile_w(inp["L0_attn_Wq"].astype(f32, copy=False))
    g["w_lk0"] = tile_w(inp["L0_attn_Wlk"].astype(f32, copy=False))
    g["w_lv0"] = tile_w(inp["L0_attn_Wlv"].astype(f32, copy=False))
    g["w_o0"] = tile_w(inp["L0_attn_Wo"].astype(f32, copy=False))
    g["w_k0"] = tile_w(inp["L0_attn_Wk"].astype(f32, copy=False))[0]
    g["w_v0"] = tile_w(inp["L0_attn_Wv"].astype(f32, copy=False))[0]
    g["w_c0"] = np.ascontiguousarray(inp["L0_attn_Wc"].reshape(16, 128).T)
    g["b_q0"] = col_form(inp["L0_attn_bq"])
    g["b_lk0"] = col_form(inp["L0_attn_blk"])
    g["b_o0"] = col_form(inp["L0_attn_bo"])
    g["b_lv0r"] = np.ascontiguousarray(inp["L0_attn_blv"].reshape(1, 2048))
    g["b_k0"] = np.ascontiguousarray(inp["L0_attn_bk"].reshape(128, 1))
    g["b_v0r"] = np.ascontiguousarray(inp["L0_attn_bv"].reshape(1, 128))
    g["b_c0"] = np.ascontiguousarray(inp["L0_attn_bc"].reshape(1, 1))
    g["qn_w"] = np.ascontiguousarray(inp["L0_attn_qn_w"].reshape(128, 1))
    g["kn_w"] = np.ascontiguousarray(inp["L0_attn_kn_w"].reshape(128, 1))
    g["sink_k"] = np.ascontiguousarray(inp["L0_attn_sink_k"].T)
    g["sink_v"] = np.ascontiguousarray(inp["L0_attn_sink_v"].reshape(1, 2048))
    g["w_g0"] = tile_w(inp["L0_ffn_w_gate"])
    g["w_u0"] = tile_w(inp["L0_ffn_w_up"])
    g["w_d0"] = tile_w(inp["L0_ffn_w_down"])
    g["s_g0"] = col_form(inp["L0_ffn_s_gate"])
    g["s_u0"] = col_form(inp["L0_ffn_s_up"])
    g["s_d0"] = col_form(inp["L0_ffn_s_down"])
    g["w_q1"] = tile_w(np.ascontiguousarray(inp["L1_attn_Wq"].T))
    g["w_k1"] = tile_w(np.ascontiguousarray(inp["L1_attn_Wk"].T))
    g["w_v1"] = tile_w(np.ascontiguousarray(inp["L1_attn_Wv"].T))
    g["w_g1"] = tile_w(np.ascontiguousarray(inp["L1_attn_Wg"].T))
    g["w_gk1"] = tile_w(np.ascontiguousarray(inp["L1_attn_Wgk"].T))[0]
    g["w_o1"] = tile_w(np.ascontiguousarray(inp["L1_attn_Wo"].T))
    g["w_g1f"] = tile_w(np.ascontiguousarray(inp["L1_ffn_gate_w"].T))
    g["w_u1f"] = tile_w(np.ascontiguousarray(inp["L1_ffn_up_w"].T))
    g["w_d1f"] = tile_w(np.ascontiguousarray(inp["L1_ffn_down_w"].T))
    g["s_g1"] = col_form(inp["L1_ffn_gate_s"])
    g["s_u1"] = col_form(inp["L1_ffn_up_s"])
    g["s_d1"] = col_form(inp["L1_ffn_down_s"])
    return g


WSPECS = {
    "w_q0": ([16, 128, 2048], F32), "w_lk0": ([16, 128, 2048], F32),
    "w_lv0": ([16, 128, 2048], F32), "w_o0": ([16, 128, 2048], F32),
    "w_k0": ([128, 2048], F32), "w_v0": ([128, 2048], F32), "w_c0": ([128, 16], F32),
    "b_q0": ([128, 16], F32), "b_lk0": ([128, 16], F32), "b_o0": ([128, 16], F32),
    "b_lv0r": ([1, 2048], F32), "b_k0": ([128, 1], F32), "b_v0r": ([1, 128], F32),
    "b_c0": ([1, 1], F32), "qn_w": ([128, 1], F32), "kn_w": ([128, 1], F32),
    "sink_k": ([128, 16], F32), "sink_v": ([1, 2048], F32),
    "w_g0": ([64, 128, 2048], I8), "w_u0": ([64, 128, 2048], I8),
    "w_d0": ([16, 128, 8192], I8),
    "s_g0": ([128, 64], F32), "s_u0": ([128, 64], F32), "s_d0": ([128, 16], F32),
    "w_q1": ([16, 128, 2048], F32), "w_k1": ([16, 128, 2048], F32),
    "w_v1": ([16, 128, 2048], F32), "w_g1": ([16, 128, 2048], F32),
    "w_gk1": ([128, 512], F32), "w_o1": ([16, 128, 2048], F32),
    "w_g1f": ([44, 128, 2048], F32), "w_u1f": ([44, 128, 2048], F32),
    "w_d1f": ([16, 128, 5632], F32),
    "s_g1": ([128, 44], F32), "s_u1": ([128, 44], F32), "s_d1": ([128, 16], F32),
}



class Ctx:
    def __init__(self):
        self.ops = {e: [] for e in ENGINES}
        self.cnt = {s: 0 for s in SEMS}

    def op(self, eng, emit, waits=(), inc=None, amt=1):
        if BARRIER_MODE and inc is None and eng in ("dve", "act", "pool", "pe"):
            inc = {"dve": "dve", "act": "act", "pool": "pool", "pe": "pe"}[eng]
        elif eng == "dve" and inc is None:
            inc = "dve"
        for sname, v in waits:
            assert v <= self.cnt[sname.rstrip("!")], \
                f"wait on future: {eng} {sname} {v} > {self.cnt[sname.rstrip('!')]}"
        snap = dict(self.cnt) if BARRIER_MODE else None
        if inc is not None:
            self.cnt[inc] += amt
        self.ops[eng].append((emit, tuple(waits), inc, amt, snap))
        return self.cnt[inc] if inc is not None else None

    def dma(self, q, out, in_, waits=(), **kw):
        sem = DMA_SEM[q]
        return self.op(q, lambda e, o=out, i=in_, k=kw: e.dma_start(out=o, in_=i, **k),
                       waits, sem, 16)

    def mm(self, out, lhsT, rhs, start=True, stop=True, waits=(), inc=None, **kw):
        return self.op("pe", lambda e, o=out, l=lhsT, r=rhs, s1=start, s2=stop, k=kw:
                       e.matmul(o, l, r, start=s1, stop=s2, **k), waits, inc)

    def tr(self, out, in_, ident, waits=(), inc=None):
        return self.op("pe", lambda e, o=out, i=in_, d=ident: e.transpose(o, i, d), waits, inc)

    def cp(self, eng, out, in_, waits=(), inc=None):
        return self.op(eng, lambda e, o=out, i=in_: e.tensor_copy(o, i), waits, inc)

    def tt(self, eng, out, in0, in1, op, waits=(), inc=None):
        return self.op(eng, lambda e, o=out, a=in0, b=in1, p=op: e.tensor_tensor(o, a, b, p),
                       waits, inc)

    def ts(self, eng, out, in0, s1, op0, s2=None, op1=ALU.mult, waits=(), inc=None):
        def emit(e, o=out, a=in0, x=s1, y=s2, p0=op0, p1=op1):
            if y is None:
                return e.tensor_scalar(o, a, x, None, p0)
            return e.tensor_scalar(o, a, x, y, p0, p1)
        return self.op(eng, emit, waits, inc)

    def stt(self, eng, out, in0, scalar, in1, op0, op1, waits=(), inc=None):
        return self.op(eng, lambda e, o=out, a=in0, s=scalar, b=in1, p0=op0, p1=op1:
                       e.scalar_tensor_tensor(o, a, s, b, p0, p1), waits, inc)

    def act(self, out, in_, func, bias=0.0, scale=1.0, accum_out=None, waits=(), inc=None):
        return self.op("act", lambda e, o=out, i=in_, f=func, b=bias, s=scale, ao=accum_out:
                       e.activation(o, i, f, bias=b, scale=s, accum_out=ao), waits, inc)

    def red(self, eng, out, in_, op, negate=False, waits=(), inc=None):
        return self.op(eng, lambda e, o=out, i=in_, p=op, n=negate:
                       e.tensor_reduce(o, i, AX.X, p, negate=n), waits, inc)

    def memset(self, eng, ap, val, waits=(), inc=None):
        return self.op(eng, lambda e, a=ap, v=val: e.memset(a, v), waits, inc)

    def shuf(self, out, in_, mask, waits=(), inc=None):
        return self.op("dve", lambda e, o=out, i=in_, m=mask: e.stream_shuffle(o, i, list(m)),
                       waits, inc)

    def recip(self, out, in_, waits=(), inc=None):
        return self.op("dve", lambda e, o=out, i=in_: e.reciprocal(o, i), waits, inc)

    def c(self, sem):
        return self.cnt[sem]

    def w(self, *pairs):
        return tuple((s, self.cnt[s] if v is None else v) for s, v in pairs)


def emit_block(nc, ctx, es):
    block = es.enter_context(nc.Block())
    sems = {s: es.enter_context(nc.semaphore(s)) for s in SEMS}
    mile = {"sync": None, "pool": "pool", "dve": "dve", "act": "act", "pe": "pe"}
    for ename in ENGINES:
        section = getattr(block, BLOCK_ATTR[ename])

        def body(engobj, _e=ename):
            last = {}
            prog = {s: 0 for s in SEMS}
            own = mile[_e]
            for emit, waits, inc, amt, snap in ctx.ops[_e]:
                if _e == "dve" and prog["dve"] > 0:
                    engobj.wait_ge(sems["dve"], prog["dve"])
                    last["dve"] = max(last.get("dve", -1), prog["dve"])
                if snap is not None:
                    for sname2 in SEMS:
                        if sname2 == mile[_e] or (_e in DMA_SEM and sname2 == DMA_SEM[_e]):
                            continue
                        v2 = snap[sname2]
                        if v2 > 0 and last.get(sname2, -1) < v2:
                            engobj.wait_ge(sems[sname2], v2)
                            last[sname2] = v2
                for sname, v in waits:
                    force = sname.endswith("!")
                    if force:
                        sname = sname[:-1]
                    if sname == own and not force:
                        assert v <= prog[sname], f"{_e} waits own future {sname} {v} > {prog[sname]}"
                        continue
                    if not force and last.get(sname, -1) >= v:
                        continue
                    engobj.wait_ge(sems[sname], v)
                    last[sname] = max(last.get(sname, -1), v)
                inst = emit(engobj)
                if inc is not None:
                    assert inst is not None, f"{_e} inc without instruction"
                    inst.then_inc(sems[inc], amt)
                    prog[inc] += amt

        section(body)
    return sems



class B:
    def __init__(self, nc, ctx, debug=()):
        self.nc = nc
        self.c = ctx
        self.debug = set(debug)
        self.dbg_tensors = {}


def carve_ap(base_ap, off_bytes, dtype, p, f):
    esz = {F32: 4, BF16: 2, I8: 1}[dtype]
    assert off_bytes % esz == 0
    v = base_ap.bitcast(dtype)
    return v[0:p, off_bytes // esz: off_bytes // esz + f]


def build_program(phases, debug=()):
    nc = bass.Bass("TRN2", target_bir_lowering=False)
    ctx = Ctx()
    b = B(nc, ctx, debug)
    b.phases = set(phases)

    dt = nc.dram_tensor
    b.x = dt("x", [S, DM], F32, kind="ExternalInput").ap()
    b.out = dt("OUT", [S, DM], F32, kind="ExternalOutput").ap()
    b.X0fm = dt("X0fm", [NCH, 128, S], F32, kind="Internal").ap()
    b.F0 = dt("F0", [NCH, 128, S], F32, kind="Internal").ap()
    b.Y0fm = dt("Y0fm", [NCH, 128, S], F32, kind="Internal").ap()
    b.F1 = dt("F1", [NCH, 128, S], F32, kind="Internal").ap()
    b.CTX0 = dt("CTX0", [NH0, 128, S], BF16, kind="Internal").ap()
    b.O1 = dt("O1", [NCH, 128, S], BF16, kind="Internal").ap()

    b.hconsts = make_consts()
    b.tin = {}
    for k, v in b.hconsts.items():
        dty = BF16 if v.dtype == BF else F32
        b.tin[k] = dt("c_" + k, list(v.shape), dty, kind="ExternalInput").ap()
    b.wt = {k: dt(k, list(shp), dty, kind="ExternalInput").ap() for k, (shp, dty) in WSPECS.items()}

    es = ExitStack()
    with es:
        sbt = lambda n, shp, dty: es.enter_context(nc.sbuf_tensor(n, shp, dty))
        b.xn = sbt("xn", [128, NCH * S], BF16)[:]
        b.arena = sbt("arena", [128, 81920], I8)[:]
        b.wta = sbt("wta", [128, 16384], I8)[:]
        b.scr = sbt("scr", [128, 4096], I8)[:]
        b.t = {}
        for k, v in b.hconsts.items():
            dty = BF16 if v.dtype == BF else F32
            b.t[k] = sbt("s_" + k, list(v.shape), dty)[:]
        SMALL = {
            "wc_b": ([128, 16], BF16), "bq": ([128, 16], F32), "blk": ([128, 16], F32),
            "bo": ([128, 16], F32), "bk": ([128, 1], F32),
            "bvr": ([1, 128], F32), "bc": ([1, 1], F32), "qn": ([128, 1], F32),
            "kn": ([128, 1], F32), "sinkk": ([128, 16], BF16),
            "blv_h": ([1, 128], F32), "sv_h": ([1, 128], BF16),
            "ck": ([128, 128], BF16), "cv": ([128, 128], BF16),
            "entr": ([128, 2048], BF16),
            "sg0": ([128, 64], F32), "su0": ([128, 64], F32), "sd0": ([128, 16], F32),
            "sg1": ([128, 44], F32), "su1": ([128, 44], F32), "sd1": ([128, 16], F32),
        }
        b.small = {k: sbt("sm_" + k, list(shp), dty)[:] for k, (shp, dty) in SMALL.items()}
        b.constk = sbt("constk", [128, 8], F32)[:]
        b.pb = [es.enter_context(nc.psum_tensor(f"pb{i}", [128, 512], F32))[:] for i in range(6)]
        b.pbh = [es.enter_context(nc.psum_tensor(f"pbh{i}", [128, 1024], BF16))[:] for i in range(2)]

        load_consts(b)
        if "ln0" in b.phases:
            emit_ln0(b)
        if "cmp" in b.phases:
            emit_cmp(b)
        if "ffn0" in b.phases:
            emit_ffn(b, 0)
        if "att0" in b.phases:
            emit_att0(b)
        if "wo0" in b.phases:
            emit_wo(b, 0)
        if "gk" in b.phases:
            emit_gk(b)
        if "ffn1" in b.phases:
            emit_ffn(b, 1)
        if "att1" in b.phases:
            emit_att1(b)
        if "wo1" in b.phases:
            emit_wo(b, 1)
        emit_debug(b)
        finalize(b)
        emit_block(nc, ctx, es)
    return nc, b


def phase_barrier(b):
    c = b.c
    b.pw = c.w(("sdma", None), ("pdma", None), ("pe", None), ("dve", None),
               ("act", None), ("pool", None))
    return b.pw


def load_consts(b):
    c = b.c
    for k in b.hconsts:
        c.dma("sync", b.t[k], b.tin[k][:, :])
    sm, wt = b.small, b.wt
    for dst, src in [("bq", "b_q0"), ("blk", "b_lk0"), ("bo", "b_o0"),
                     ("bk", "b_k0"), ("bvr", "b_v0r"), ("bc", "b_c0"), ("qn", "qn_w"),
                     ("kn", "kn_w"), ("sg0", "s_g0"), ("su0", "s_u0"), ("sd0", "s_d0"),
                     ("sg1", "s_g1"), ("su1", "s_u1"), ("sd1", "s_d1")]:
        c.dma("sync", sm[dst], wt[src][:, :])
    c.dma("pool", sm["wc_b"], wt["w_c0"][:, :])
    c.dma("pool", sm["sinkk"], wt["sink_k"][:, :])
    for col, val in [(0, 1e-5), (1, 1e-6), (2, -0.5 * LN128), (3, float(np.log(0.125)))]:
        c.memset("pool", b.constk[:, col:col + 1], val, inc="pool")
    b.consts_loaded = c.w(("sdma", None), ("pdma", None), ("pool", None))



def emit_ln0(b):
    c = b.c
    idf = b.t["idf"]
    onesc = b.t["ones_col_f"]
    onesr = b.t["ones_row_f"]
    X0g = carve_ap(b.arena, 0, F32, 128, 8192)
    xtm = [carve_ap(b.arena, 32768 + i * 8192, F32, 128, 2048) for i in range(2)]
    rows = carve_ap(b.arena, 49152, F32, 1, 1536)
    sq = carve_ap(b.scr, 0, F32, 128, 512)
    tmp = carve_ap(b.scr, 2048, F32, 128, 512)
    pT = [b.pb[0], b.pb[1]]
    pS, pS2, pM, pR = b.pb[2], b.pb[3], b.pb[4], b.pb[5]

    lw = b.consts_loaded
    pe_done = {}
    evac = []
    store_prev = 0
    for tg in range(NTG):
        for tt in range(4):
            tti = tg * 4 + tt
            buf = xtm[tti % 2]
            waits = list(lw)
            if tti >= 2:
                waits.append(("pe", pe_done[tti - 2]))
            dn = c.dma("sync", buf, b.x[tti * 128:(tti + 1) * 128, :], waits=tuple(waits))
            for ci in range(NCH):
                bank = pT[ci % 2]
                trw = [("sdma", dn)]
                if len(evac) >= 2:
                    trw.append(("dve", evac[-2]))
                c.tr(bank[:, 0:128], buf[:, ci * 128:(ci + 1) * 128], idf,
                     waits=tuple(trw), inc="pe")
                ev = c.cp("dve", X0g[:, ci * 512 + tt * 128: ci * 512 + tt * 128 + 128],
                          bank[:, 0:128],
                          waits=c.w(("pe", None)) + (("sdma", store_prev),), inc="dve")
                evac.append(ev)
            pe_done[tti] = c.c("pe")
        for ci in range(NCH):
            xsl = X0g[:, ci * 512:(ci + 1) * 512]
            c.act(sq, xsl, AF.Square, waits=c.w(("dve", None), ("pe", None)), inc="act")
            c.mm(pS[0:1, 0:512], onesc, xsl, start=(ci == 0), stop=(ci == NCH - 1),
                 waits=c.w(("dve", None), ("act", None)), inc="pe")
            c.mm(pS2[0:1, 0:512], onesc, sq, start=(ci == 0), stop=(ci == NCH - 1),
                 waits=c.w(("act", None)), inc="pe")
        sx = rows[0:1, 0:512]
        sx2 = rows[0:1, 512:1024]
        wk = rows[0:1, 1024:1536]
        c.ts("dve", sx, pS[0:1, 0:512], 1.0 / DM, ALU.mult, waits=c.w(("pe", None)), inc="dve")
        c.ts("dve", sx2, pS2[0:1, 0:512], 1.0 / DM, ALU.mult, inc="dve")
        c.tt("dve", wk, sx, sx, ALU.mult, inc="dve")
        c.tt("dve", sx2, sx2, wk, ALU.subtract, inc="dve")
        c.act(wk, sx2, AF.Ln, bias=b.constk[0:1, 0:1], scale=1.0,
              waits=c.w(("dve", None)), inc="act")
        c.act(wk, wk, AF.Exp, scale=-0.5, inc="act")
        c.mm(pM[:, 0:512], onesr, sx, waits=c.w(("act", None), ("dve", None)), inc="pe")
        c.mm(pR[:, 0:512], onesr, wk, inc="pe")
        mmn = c.c("pe")
        for ci in range(NCH):
            xsl = X0g[:, ci * 512:(ci + 1) * 512]
            c.tt("dve", tmp, xsl, pM[:, 0:512], ALU.subtract, waits=(("pe", mmn),), inc="dve")
            c.tt("dve", b.xn[:, ci * S + tg * 512: ci * S + tg * 512 + 512], tmp,
                 pR[:, 0:512], ALU.mult, inc="dve")
        store_prev = c.dma("sync",
                           b.X0fm[:, :, tg * 512:(tg + 1) * 512].rearrange("ch p t -> p ch t"),
                           X0g.rearrange("p (ch t) -> p ch t", ch=NCH),
                           waits=c.w(("dve", None)))
    b.ln0_xn = c.c("dve")
    b.ln0_store = store_prev



def emit_cmp(b):
    c = b.c
    pw = phase_barrier(b)
    sm = b.small
    onesc = b.t["ones_col_f"]
    onesr = b.t["ones_row_f"]
    wb = carve_ap(b.arena, 0, BF16, 128, 2048)
    tmpf = carve_ap(b.arena, 8192, F32, 128, 2048)
    lg = carve_ap(b.arena, 16384, F32, 1, 2048)
    erow = carve_ap(b.arena, 24576, F32, 1, 2048)
    wrow = carve_ap(b.arena, 32768, F32, 1, 2048)
    mrow = carve_ap(b.arena, 40960, F32, 1, 128)
    srow = carve_ap(b.arena, 41472, F32, 1, 128)
    vrow = carve_ap(b.arena, 41984, F32, 1, 128)
    rrow = carve_ap(b.arena, 42496, F32, 1, 128)
    ckf = carve_ap(b.scr, 0, F32, 128, 128)
    sqc = carve_ap(b.scr, 512, F32, 128, 128)
    swp = carve_ap(b.scr, 1024, BF16, 64, 128)
    t1 = carve_ap(b.scr, 1280, BF16, 64, 128)
    t2 = carve_ap(b.scr, 1536, BF16, 64, 128)
    ered = carve_ap(b.scr, 2048, F32, 128, 128)

    xn_ready = b.ln0_xn
    lw = b.consts_loaded
    pL = b.pb[0]
    for tg in range(NTG):
        for ci in range(NCH):
            c.mm(pL[0:1, 0:512], sm["wc_b"][:, ci:ci + 1],
                 b.xn[:, ci * S + tg * 512: ci * S + tg * 512 + 512],
                 start=(ci == 0), stop=(ci == NCH - 1),
                 waits=(("dve", xn_ready),) + lw + c.w(("act", None)),
                 inc="pe" if ci == NCH - 1 else None)
        c.act(lg[0:1, tg * 512:(tg + 1) * 512], pL[0:1, 0:512], AF.Identity,
              bias=sm["bc"][0:1, 0:1], waits=pw + c.w(("pe", None)), inc="act")
    lg3 = lg.rearrange("p (blk r) -> p blk r", r=16)
    c.red("dve", mrow, lg3, ALU.max, negate=True, waits=c.w(("act", None)), inc="dve")
    mb = mrow.rearrange("p (blk one) -> p blk one", one=1).broadcast_to([1, 128, 16])
    er3 = erow.rearrange("p (blk r) -> p blk r", r=16)
    c.tt("dve", er3, lg3, mb, ALU.add, inc="dve")
    c.act(erow, erow, AF.Exp, waits=c.w(("dve", None)), inc="act")
    c.red("dve", srow, er3, ALU.add, waits=c.w(("act", None)), inc="dve")
    c.recip(srow, srow, inc="dve")
    sb3 = srow.rearrange("p (blk one) -> p blk one", one=1).broadcast_to([1, 128, 16])
    wr3 = wrow.rearrange("p (blk r) -> p blk r", r=16)
    c.tt("dve", wr3, er3, sb3, ALU.mult, inc="dve")
    wn = c.c("dve")
    for tg in range(NTG):
        c.mm(b.pb[1][:, 0:512], onesr, wrow[0:1, tg * 512:(tg + 1) * 512],
             waits=(("dve", wn),) + c.w(("dve", None)), inc="pe")
        c.cp("dve", wb[:, tg * 512:(tg + 1) * 512], b.pb[1][:, 0:512],
             waits=pw + c.w(("pe", None)), inc="dve")
    for ci in range(NCH):
        c.tt("dve", tmpf, b.xn[:, ci * S:(ci + 1) * S], wb, ALU.mult, waits=pw, inc="dve")
        c.red("dve", ered, tmpf.rearrange("p (blk r) -> p blk r", r=16), ALU.add, inc="dve")
        c.cp("dve", sm["entr"][:, ci * 128:(ci + 1) * 128], ered, inc="dve")
    entr_n = c.c("dve")
    wk_ = carve_ap(b.wta, 0, BF16, 128, 2048)
    wv_ = carve_ap(b.wta, 4096, BF16, 128, 2048)
    k1 = c.dma("pool", wk_, b.wt["w_k0"][:, :])
    k2 = c.dma("pool", wv_, b.wt["w_v0"][:, :])
    pCK = b.pb[2]
    for ci in range(NCH):
        c.mm(pCK[:, 0:128], wk_[:, ci * 128:(ci + 1) * 128], sm["entr"][:, ci * 128:(ci + 1) * 128],
             start=(ci == 0), stop=(ci == NCH - 1),
             waits=(("dve", entr_n), ("pdma", k1)), inc="pe" if ci == NCH - 1 else None)
    c.act(ckf, pCK[:, 0:128], AF.Identity, bias=sm["bk"][:, 0:1], waits=c.w(("pe", None)), inc="act")
    c.act(sqc, ckf, AF.Square, inc="act")
    c.mm(b.pb[3][0:1, 0:128], onesc, sqc, waits=c.w(("act", None)), inc="pe")
    c.act(vrow, b.pb[3][0:1, 0:128], AF.Ln, bias=b.constk[0:1, 1:2], scale=1.0 / 128,
          waits=c.w(("pe", None)), inc="act")
    c.act(rrow, vrow, AF.Exp, scale=-0.5, inc="act")
    c.mm(b.pb[3][:, 128:256], onesr, rrow, waits=c.w(("act", None)), inc="pe")
    c.stt("dve", sm["ck"], ckf, sm["kn"][:, 0:1], b.pb[3][:, 128:256], ALU.mult, ALU.mult,
          waits=c.w(("pe", None), ("act", None)), inc="dve")
    ck = sm["ck"]
    c.cp("dve", swp[0:32, :], ck[32:64, 0:128], inc="dve")
    c.cp("dve", swp[32:64, :], ck[0:32, 0:128], inc="dve")
    c.tt("dve", t1, ck[0:64, 0:128], b.t["ckcos"], ALU.mult, inc="dve")
    c.tt("dve", t2, swp, b.t["ckss"], ALU.mult, inc="dve")
    c.tt("dve", ck[0:64, 0:128], t1, t2, ALU.add, inc="dve")
    pCV = b.pb[4]
    for ci in range(NCH):
        c.mm(pCV[:, 0:128], sm["entr"][:, ci * 128:(ci + 1) * 128], wv_[:, ci * 128:(ci + 1) * 128],
             start=(ci == 0), stop=False, waits=(("dve", entr_n), ("pdma", k2)))
    c.mm(pCV[:, 0:128], onesr, sm["bvr"][0:1, 0:128], start=False, stop=True,
         waits=lw, inc="pe")
    c.cp("dve", sm["cv"], pCV[:, 0:128], waits=c.w(("pe", None)), inc="dve")
    b.cmp_done = c.c("dve")




FFN_NQ = None
FFN_SKIP_B = False


def emit_ffn(b, layer):
    c = b.c
    pw = phase_barrier(b)
    if layer == 0:
        nI, NQ = I0CH, 4
        wg, wu, wd = b.wt["w_g0"], b.wt["w_u0"], b.wt["w_d0"]
        sg, su, sd = b.small["sg0"], b.small["su0"], b.small["sd0"]
        F = b.F0
    else:
        nI, NQ = I1CH, 4
        wg, wu, wd = b.wt["w_g1f"], b.wt["w_u1f"], b.wt["w_d1f"]
        sg, su, sd = b.small["sg1"], b.small["su1"], b.small["sd1"]
        F = b.F1
    Q = nI // NQ
    if FFN_NQ is not None:
        NQ = FFN_NQ
    hq = carve_ap(b.arena, 0, BF16, 128, Q * 2048)
    slab0 = Q * 4096
    gsl = [carve_ap(b.arena, slab0 + i * 4096, BF16, 128, 2048) for i in range(2)]
    usl = [carve_ap(b.arena, slab0 + 8192 + i * 4096, BF16, 128, 2048) for i in range(2)]
    dsl = [carve_ap(b.arena, slab0 + i * 4096, BF16, 128, Q * 128) for i in range(2)]
    dout = [carve_ap(b.arena, slab0 + 8192 + i * 2048, F32, 128, 512) for i in range(2)]
    hg = carve_ap(b.scr, 0, BF16, 128, 512)
    uu = carve_ap(b.scr, 1024, BF16, 128, 512)

    pg_use = {}
    pu_use = {}
    pd_evac = {}
    slab_use = {}
    dsl_use = {}
    dout_dma = {}
    last_fstore = 0
    for qi in range(NQ):
        qw = c.w(("pe", None), ("dve", None), ("act", None)) + pw
        for mi in range(Q):
            m = qi * Q + mi
            sw = list(qw)
            if mi >= 2:
                sw.append(("pe", slab_use[mi - 2]))
            if qi > 0:
                sw.append(("pdma", last_fstore))
            g_d = c.dma("pool", gsl[mi % 2], wg[m, :, :], waits=tuple(sw))
            u_d = c.dma("pool", usl[mi % 2], wu[m, :, :], waits=tuple(sw))
            for tg in range(NTG):
                it = mi * NTG + tg
                bg = b.pb[it % 2]
                bu = b.pb[2 + it % 2]
                mmw = [("pdma", g_d)]
                if it >= 2:
                    mmw.append(("act", pg_use[it - 2]))
                for ci in range(NCH):
                    c.mm(bg[:, 0:512], gsl[mi % 2][:, ci * 128:(ci + 1) * 128],
                         b.xn[:, ci * S + tg * 512: ci * S + tg * 512 + 512],
                         start=(ci == 0), stop=(ci == NCH - 1),
                         waits=tuple(mmw) + qw, inc="pe" if ci == NCH - 1 else None)
                mmw2 = [("pdma", u_d)]
                if it >= 2:
                    mmw2.append(("dve", pu_use[it - 2]))
                for ci in range(NCH):
                    c.mm(bu[:, 0:512], usl[mi % 2][:, ci * 128:(ci + 1) * 128],
                         b.xn[:, ci * S + tg * 512: ci * S + tg * 512 + 512],
                         start=(ci == 0), stop=(ci == NCH - 1),
                         waits=tuple(mmw2) + qw, inc="pe" if ci == NCH - 1 else None)
                pen = c.c("pe")
                pg_use[it] = c.act(hg, bg[:, 0:512], AF.Silu, scale=sg[:, m:m + 1],
                                   waits=(("pe", pen),) + c.w(("dve", None)), inc="act")
                pu_use[it] = c.ts("dve", uu, bu[:, 0:512], su[:, m:m + 1], ALU.mult,
                                  waits=(("pe", pen),), inc="dve")
                c.tt("dve", hq[:, mi * 2048 + tg * 512: mi * 2048 + tg * 512 + 512],
                     hg, uu, ALU.mult, waits=c.w(("act", None)), inc="dve")
            slab_use[mi] = c.c("pe")
        hq_done = c.c("dve")
        if FFN_SKIP_B:
            continue
        bw = c.w(("dve", hq_done), ("act", None))
        for hid in range(NCH):
            sw = list(bw)
            if hid >= 2:
                sw.append(("pe", dsl_use[hid - 2]))
            d_d = c.dma("pool", dsl[hid % 2], wd[hid, :, qi * Q * 128:(qi + 1) * Q * 128],
                        waits=tuple(sw))
            for tg in range(NTG):
                jt = hid * NTG + tg
                bd = b.pb[4 + jt % 2]
                mmw = [("pdma", d_d)]
                if jt >= 2:
                    mmw.append(("dve", pd_evac[jt - 2]))
                for mi in range(Q):
                    c.mm(bd[:, 0:512], dsl[hid % 2][:, mi * 128:(mi + 1) * 128],
                         hq[:, mi * 2048 + tg * 512: mi * 2048 + tg * 512 + 512],
                         start=(mi == 0), stop=(mi == Q - 1),
                         waits=tuple(mmw) + bw, inc="pe" if mi == Q - 1 else None)
                dw = [("pe", c.c("pe"))]
                if jt >= 2:
                    dw.append(("pdma", dout_dma[jt - 2]))
                pd_evac[jt] = c.ts("dve", dout[jt % 2], bd[:, 0:512], sd[:, hid:hid + 1],
                                   ALU.mult, waits=tuple(dw), inc="dve")
                stw = [("dve", pd_evac[jt])]
                if qi > 0:
                    stw.append(("pdma", last_fstore))
                dout_dma[jt] = c.dma("pool", F[hid, :, tg * 512:(tg + 1) * 512],
                                     dout[jt % 2],
                                     waits=tuple(stw),
                                     **({"accum_op": ALU.add} if qi > 0 else {}))
            dsl_use[hid] = c.c("pe")
        last_fstore = c.c("pdma")
    b.ffn_done = c.w(("pdma", None))




ATT0_HEADS = NH0


def emit_att0(b):
    c = b.c
    pw = phase_barrier(b)
    sm = b.small
    onesc = b.t["ones_col_f"]
    onesr = b.t["ones_row_f"]
    idb = b.t["idb"]
    qh = carve_ap(b.arena, 0, BF16, 128, 2048)
    lkb = carve_ap(b.arena, 4096, BF16, 128, 2176)
    lvt = carve_ap(b.arena, 8448, BF16, 128, 2176)
    ctxh = carve_ap(b.arena, 12800, BF16, 128, 2048)
    qsl = [carve_ap(b.arena, 16896 + i * 4096, BF16, 128, 2048) for i in range(2)]
    lksl = [carve_ap(b.arena, 25088 + i * 4096, BF16, 128, 2048) for i in range(2)]
    lvsl = [carve_ap(b.arena, 33280 + i * 4096, BF16, 128, 2048) for i in range(2)]
    scl = carve_ap(b.arena, 41472, F32, 128, 256)
    scc = carve_ap(b.arena, 42496, F32, 128, 132)
    el = carve_ap(b.arena, 43520, BF16, 128, 256)
    eln = carve_ap(b.arena, 44032, BF16, 128, 256)
    ec = carve_ap(b.arena, 44544, BF16, 128, 132)
    ecn = carve_ap(b.arena, 45056, BF16, 128, 132)
    atl = carve_ap(b.arena, 45568, BF16, 128, 256)
    atc = carve_ap(b.arena, 46080, BF16, 128, 128)
    srow = carve_ap(b.arena, 46336, BF16, 1, 128)
    qf32 = carve_ap(b.arena, 47104, F32, 128, 512)
    sqf = carve_ap(b.arena, 49152, F32, 128, 512)
    rsw = carve_ap(b.arena, 51200, BF16, 64, 512)
    rt1 = carve_ap(b.arena, 52224, BF16, 64, 512)
    rt2 = carve_ap(b.arena, 53248, BF16, 64, 512)
    rows = carve_ap(b.arena, 54272, F32, 1, 1024)
    mxl = carve_ap(b.arena, 58368, F32, 128, 1)
    sml = carve_ap(b.arena, 58372 + 12, F32, 128, 1)
    mxc = carve_ap(b.arena, 58400, F32, 128, 1)
    smc = carve_ap(b.arena, 58416, F32, 128, 1)
    rl_ = carve_ap(b.arena, 58432, F32, 128, 1)
    rc_ = carve_ap(b.arena, 58448, F32, 128, 1)

    pP, pSt, pSL, pSC, pCP, pLV = b.pb[0], b.pb[1], b.pb[2], b.pb[3], b.pb[4], b.pb[5]
    phL, phC = b.pbh[0], b.pbh[1]

    c.memset("dve", lkb[:, 0:128], 0.0, waits=pw, inc="dve")
    c.memset("dve", lvt[:, 0:128], 0.0, inc="dve")

    slab_use = {}
    row_use = {}
    prev = dict(proj_evac=0, stats_rd=0, bcast_rd=0, sl_rd=0, sc_rd=0, cp_rd=0,
                tpl_rd=0, tpc_rd=0, eln_rd=0, ecn_rd=0, lv_evac=0, ctx_dma=0, exps=0)

    def proj_norm(dst, slab, bias_col, nw_col, tg, do_norm):
        pen0 = None
        for ci in range(NCH):
            c.mm(pP[:, 0:512], slab[:, ci * 128:(ci + 1) * 128],
                 b.xn[:, ci * S + tg * 512: ci * S + tg * 512 + 512],
                 start=(ci == 0), stop=(ci == NCH - 1),
                 waits=(("act", prev["proj_evac"]),) + pw + c.w(("pdma", None)),
                 inc="pe" if ci == NCH - 1 else None)
        pen = c.c("pe")
        prev["proj_evac"] = c.act(qf32, pP[:, 0:512], AF.Identity, bias=bias_col,
                                  waits=(("pe", pen),) + c.w(("dve", None)), inc="act")
        c.act(sqf, qf32, AF.Square, waits=c.w(("pe", prev["stats_rd"]) if False else ("pe", prev["stats_rd"]),), inc="act")
        c.mm(pSt[0:1, 0:512], onesc, sqf, waits=c.w(("act", None), ("dve", prev["bcast_rd"])), inc="pe")
        prev["stats_rd"] = c.c("pe")
        c.act(rows[0:1, 0:512], pSt[0:1, 0:512], AF.Ln, bias=b.constk[0:1, 1:2],
              scale=1.0 / 128, waits=c.w(("pe", None)), inc="act")
        c.act(rows[0:1, 512:1024], rows[0:1, 0:512], AF.Exp, scale=-0.5,
              bias=b.constk[0:1, 2:3] if do_norm == "q" else 0.0, inc="act")
        c.mm(pSt[:, 0:512], onesr, rows[0:1, 512:1024], waits=c.w(("act", None)), inc="pe")
        bn = c.c("pe")
        sl = slice(tg * 512, tg * 512 + 512)
        prev["bcast_rd"] = c.stt("dve", dst[:, sl], qf32, nw_col, pSt[:, 0:512],
                                 ALU.mult, ALU.mult, waits=(("pe", bn),), inc="dve")
        c.cp("dve", rsw[0:32, :], dst[32:64, sl], inc="dve")
        c.cp("dve", rsw[32:64, :], dst[0:32, sl], inc="dve")
        c.tt("dve", rt1, dst[0:64, sl], b.t["l0cos"][:, tg * 512:(tg + 1) * 512], ALU.mult, inc="dve")
        c.tt("dve", rt2, rsw, b.t["l0ss"][:, tg * 512:(tg + 1) * 512], ALU.mult, inc="dve")
        c.tt("dve", dst[0:64, sl], rt1, rt2, ALU.add, inc="dve")

    for h in range(ATT0_HEADS):
        sw = list(pw)
        if h >= 2:
            sw.append(("pe", slab_use[h - 2]))
        dq = c.dma("pool", qsl[h % 2], b.wt["w_q0"][h, :, :], waits=tuple(sw))
        dlk = c.dma("pool", lksl[h % 2], b.wt["w_lk0"][h, :, :], waits=tuple(sw))
        dlv = c.dma("pool", lvsl[h % 2], b.wt["w_lv0"][h, :, :], waits=tuple(sw))
        rw = list(pw)
        if h >= 1:
            rw.append(("pe", row_use[h - 1]))
        dblv = c.dma("sync", sm["blv_h"], b.wt["b_lv0r"][0:1, h * 128:(h + 1) * 128],
                     waits=tuple(rw))
        dsv = c.dma("pool", sm["sv_h"], b.wt["sink_v"][0:1, h * 128:(h + 1) * 128],
                    waits=tuple(rw))
        for tg in range(NTG):
            c.op("pe", lambda e: e.nop(), waits=(("pdma", dq),))
            proj_norm(qh, qsl[h % 2], sm["bq"][:, h:h + 1], sm["qn"][:, 0:1], tg, "q")
        for tg in range(NTG):
            c.op("pe", lambda e: e.nop(), waits=(("pdma", dlk),))
            proj_norm(lkb[:, 128:2176], lksl[h % 2], sm["blk"][:, h:h + 1], sm["kn"][:, 0:1], tg, "k")
        for tt in range(NTT):
            for ci in range(NCH):
                c.mm(pLV[:, 0:128], b.xn[:, ci * S + tt * 128: ci * S + tt * 128 + 128],
                     lvsl[h % 2][:, ci * 128:(ci + 1) * 128],
                     start=(ci == 0), stop=False,
                     waits=(("pdma", dlv), ("sdma", dblv), ("dve", prev["lv_evac"])))
            c.mm(pLV[:, 0:128], onesr, sm["blv_h"], start=False, stop=True, inc="pe")
            prev["lv_evac"] = c.cp("dve", lvt[:, (tt + 1) * 128:(tt + 2) * 128], pLV[:, 0:128],
                                   waits=c.w(("pe", None)), inc="dve")
        qlk_ready = c.c("dve")
        for qt in range(NTT):
            mask_sel = b.t["mask_loc0"] if qt == 0 else b.t["mask_loc"]
            c.mm(pSL[:, 0:256], qh[:, qt * 128:(qt + 1) * 128],
                 lkb[:, qt * 128: qt * 128 + 256],
                 waits=(("dve", qlk_ready), ("dve", prev["sl_rd"])), inc="pe")
            prev["sl_rd"] = c.tt("dve", scl, pSL[:, 0:256], mask_sel, ALU.add,
                                 waits=c.w(("pe", None)) + (("act", prev["exps"]),), inc="dve")
            c.red("dve", mxl, scl, ALU.max, negate=True, inc="dve")
            c.act(el, scl, AF.Exp, bias=mxl,
                  waits=c.w(("dve", None)), inc="act")
            n1 = c.red("dve", sml, el, ALU.add, waits=c.w(("act", None)), inc="dve")
            n2 = c.recip(rl_, sml, waits=(("dve!", n1),), inc="dve")
            c.op("dve", lambda e: e.engine_nop(), waits=(("dve!", n2),), inc="dve")
            c.ts("dve", eln, el, rl_, ALU.mult, 0.5, ALU.mult,
                 waits=(("pe", prev["eln_rd"]),), inc="dve")
            c.mm(pSC[:, 0:128], qh[:, qt * 128:(qt + 1) * 128], sm["ck"],
                 waits=(("dve", qlk_ready), ("dve", prev["sc_rd"]), ("pdma", dsv)), inc="pe")
            c.mm(pSC[:, 128:129], qh[:, qt * 128:(qt + 1) * 128], sm["sinkk"][:, h:h + 1],
                 inc="pe")
            prev["sc_rd"] = c.tt("dve", scc[:, 0:128], pSC[:, 0:128],
                                 b.t["cmaskb"][:, 128 - 8 * qt: 256 - 8 * qt], ALU.add,
                                 waits=c.w(("pe", None)) + (("act", prev["exps"]),), inc="dve")
            prev["sc_rd"] = c.cp("dve", scc[:, 128:129], pSC[:, 128:129], inc="dve")
            c.red("dve", mxc, scc[:, 0:129], ALU.max, negate=True, inc="dve")
            c.act(ec[:, 0:129], scc[:, 0:129], AF.Exp, bias=mxc,
                  waits=c.w(("dve", None)), inc="act")
            m1 = c.red("dve", smc, ec[:, 0:129], ALU.add, waits=c.w(("act", None)), inc="dve")
            m2 = c.recip(rc_, smc, waits=(("dve!", m1),), inc="dve")
            c.op("dve", lambda e: e.engine_nop(), waits=(("dve!", m2),), inc="dve")
            c.ts("dve", ecn[:, 0:129], ec[:, 0:129], rc_, ALU.mult, 0.5, ALU.mult,
                 waits=(("pe", prev["ecn_rd"]),), inc="dve")
            prev["exps"] = c.c("act")
            dv_ready = c.c("dve")
            c.tr(phL[:, 0:128], eln[:, 0:128], idb, waits=(("dve", dv_ready), ("dve", prev["tpl_rd"])))
            c.tr(phL[:, 128:256], eln[:, 128:256], idb, inc="pe")
            prev["eln_rd"] = c.c("pe")
            c.tr(phC[:, 0:128], ecn[:, 0:128], idb, waits=(("dve", prev["tpc_rd"]),))
            c.tr(phC[0:1, 128:256], ecn[:, 128:129], idb, inc="pe")
            prev["ecn_rd"] = c.c("pe")
            prev["tpl_rd"] = c.cp("dve", atl, phL[:, 0:256], waits=c.w(("pe", None)), inc="dve")
            c.cp("dve", atc, phC[:, 0:128], inc="dve")
            prev["tpc_rd"] = c.cp("dve", srow, phC[0:1, 128:256], inc="dve")
            tp_ev = c.c("dve")
            c.mm(pCP[:, 0:128], lvt[:, qt * 128:(qt + 1) * 128], atl[:, 0:128],
                 start=True, stop=False, waits=(("dve", tp_ev), ("dve", prev["cp_rd"])))
            c.mm(pCP[:, 0:128], lvt[:, (qt + 1) * 128:(qt + 2) * 128], atl[:, 128:256],
                 start=False, stop=False)
            c.mm(pCP[:, 0:128], sm["cv"], atc, start=False, stop=False)
            c.mm(pCP[:, 0:128], sm["sv_h"], srow, start=False, stop=True, inc="pe")
            prev["cp_rd"] = c.cp("dve", ctxh[:, qt * 128:(qt + 1) * 128], pCP[:, 0:128],
                                 waits=c.w(("pe", None)) + (("sdma", prev["ctx_dma"]),), inc="dve")
        row_use[h] = c.c("pe")
        slab_use[h] = c.c("pe")
        prev["ctx_dma"] = c.dma("sync", b.CTX0[h, :, :], ctxh, waits=c.w(("dve", None)))
    b.att0_done = c.c("dve")
    b.att0_taps = {"qh": qh, "lkb": lkb, "lvt": lvt, "scl": scl, "eln": eln,
                   "atl": atl, "ecn": ecn, "atc": atc, "scc": scc}


def emit_wo(b, layer):
    c = b.c
    pw = phase_barrier(b)
    sm = b.small
    onesc = b.t["ones_col_f"]
    onesr = b.t["ones_row_f"]
    idf = b.t["idf"]
    ctg = carve_ap(b.arena, 0, BF16, 128, 8192)
    wosl = [carve_ap(b.arena, 16384 + i * 4096, BF16, 128, 2048) for i in range(2)]
    at = carve_ap
    if layer == 0:
        f0t = [at(b.arena, 24576 + i * 2048, F32, 128, 512) for i in range(2)]
        x0t = [at(b.arena, 28672 + i * 2048, F32, 128, 512) for i in range(2)]
        ysb = at(b.arena, 32768, F32, 128, 512)
        tmp2 = at(b.arena, 34816, F32, 128, 512)
        y2 = [at(b.arena, 36864 + i * 2048, F32, 128, 512) for i in range(2)]
        sq2 = at(b.arena, 40960, F32, 128, 512)
        rows = at(b.arena, 43008, F32, 1, 1024)
        meanr = at(b.arena, 47104, F32, 1, 2048)
        rstdr = at(b.arena, 55296, F32, 1, 2048)
        ytl = [at(b.arena, 63488 + i * 2048, F32, 128, 512) for i in range(2)]
        tmp3 = at(b.arena, 67584, F32, 128, 512)
        W, SRC = b.wt["w_o0"], b.CTX0
        pS1, pS2, pM, pR = b.pb[2], b.pb[3], b.pb[4], b.pb[5]
    else:
        f1t = [at(b.arena, 24576 + i * 512, F32, 128, 128) for i in range(2)]
        y0t = [at(b.arena, 25600 + i * 512, F32, 128, 128) for i in range(2)]
        ysb = at(b.arena, 26624, F32, 128, 128)
        tmp2 = at(b.arena, 27136, F32, 128, 128)
        y2 = at(b.arena, 27648, F32, 128, 128)
        otm = [at(b.arena, 32768 + i * 8192, F32, 128, 2048) for i in range(4)]
        W, SRC = b.wt["w_o1"], b.O1
        pT = b.pb[5]
    pwo = [b.pb[0], b.pb[1]]

    slab_use = {}
    evac = {}
    tile_use = {}
    st_cnt = 0
    for tg in range(NTG):
        dctg = c.dma("sync", ctg.rearrange("p (h t) -> p h t", h=16),
                     SRC[:, :, tg * 512:(tg + 1) * 512].rearrange("h p t -> p h t"),
                     waits=pw + c.w(("pe", None)))
        if layer == 0:
            for m in range(NCH):
                sw = list(pw)
                if m >= 2:
                    sw.append(("pe", slab_use[m - 2]))
                dwo = c.dma("pool", wosl[m % 2], W[m, :, :], waits=tuple(sw))
                lw = list(pw)
                if m >= 2:
                    lw.append(("dve", evac[m - 2]))
                df0 = c.dma("sync", f0t[m % 2], b.F0[m, :, tg * 512:(tg + 1) * 512],
                            waits=tuple(lw) + (b.ffn_done if hasattr(b, "ffn_done") else ()))
                dx0 = c.dma("sync", x0t[m % 2], b.X0fm[m, :, tg * 512:(tg + 1) * 512],
                            waits=tuple(lw))
                for hh in range(NH0):
                    c.mm(pwo[m % 2][:, 0:512], wosl[m % 2][:, hh * 128:(hh + 1) * 128],
                         ctg[:, hh * 512:(hh + 1) * 512],
                         start=(hh == 0), stop=(hh == NH0 - 1),
                         waits=(("pdma", dwo), ("sdma", dctg),
                                ("act", evac.get(("a", m - 2), 0))),
                         inc="pe" if hh == NH0 - 1 else None)
                evac[("a", m)] = c.act(ysb, pwo[m % 2][:, 0:512], AF.Identity,
                                       bias=sm["bo"][:, m:m + 1],
                                       waits=c.w(("pe", None), ("dve", None)), inc="act")
                c.tt("dve", tmp2, ysb, f0t[m % 2], ALU.add,
                     waits=c.w(("act", None)) + (("sdma", df0), ("pe", evac.get(("s", m - 2), 0))), inc="dve")
                evac[m] = c.tt("dve", y2[m % 2], tmp2, x0t[m % 2], ALU.add,
                               waits=(("sdma", dx0), ("sdma", st_cnt)), inc="dve")
                st_cnt = c.dma("sync", b.Y0fm[m, :, tg * 512:(tg + 1) * 512], y2[m % 2],
                               waits=c.w(("dve", None)))
                c.act(sq2, y2[m % 2], AF.Square, waits=c.w(("dve", None), ("pe", None)), inc="act")
                c.mm(pS1[0:1, 0:512], onesc, y2[m % 2], start=(m == 0), stop=(m == NCH - 1),
                     waits=c.w(("dve", None), ("dve", None)), inc="pe")
                evac[("s", m)] = c.c("pe")
                c.mm(pS2[0:1, 0:512], onesc, sq2, start=(m == 0), stop=(m == NCH - 1),
                     waits=c.w(("act", None)), inc="pe")
                slab_use[m] = c.c("pe")
            sx = rows[0:1, 0:512]
            sx2 = rows[0:1, 512:1024]
            mslice = meanr[0:1, tg * 512:(tg + 1) * 512]
            rslice = rstdr[0:1, tg * 512:(tg + 1) * 512]
            c.ts("dve", mslice, pS1[0:1, 0:512], 1.0 / DM, ALU.mult,
                 waits=c.w(("pe", None)), inc="dve")
            c.ts("dve", sx2, pS2[0:1, 0:512], 1.0 / DM, ALU.mult, inc="dve")
            c.tt("dve", sx, mslice, mslice, ALU.mult, inc="dve")
            c.tt("dve", sx2, sx2, sx, ALU.subtract, inc="dve")
            c.act(rslice, sx2, AF.Ln, bias=b.constk[0:1, 0:1], scale=1.0,
                  waits=c.w(("dve", None)), inc="act")
            c.act(rslice, rslice, AF.Exp, scale=-0.5, inc="act")
            c.mm(pM[:, 0:512], onesr, mslice, waits=c.w(("act", None), ("dve", None)), inc="pe")
            c.mm(pR[:, 0:512], onesr, rslice, inc="pe")
            bn = c.c("pe")
            for m in range(NCH):
                lw2 = [("sdma", st_cnt)]
                if m >= 2:
                    lw2.append(("dve", tile_use[m - 2]))
                dyt = c.dma("sync", ytl[m % 2], b.Y0fm[m, :, tg * 512:(tg + 1) * 512],
                            waits=tuple(lw2))
                c.tt("dve", tmp3, ytl[m % 2], pM[:, 0:512], ALU.subtract,
                     waits=(("sdma", dyt), ("pe", bn)), inc="dve")
                tile_use[m] = c.tt("dve", b.xn[:, m * S + tg * 512: m * S + tg * 512 + 512],
                                   tmp3, pR[:, 0:512], ALU.mult, inc="dve")
        else:
            for m in range(NCH):
                sw = list(pw)
                if m >= 2:
                    sw.append(("pe", slab_use[m - 2]))
                dwo = c.dma("pool", wosl[m % 2], W[m, :, :], waits=tuple(sw))
                for t4 in range(4):
                    tti = tg * 4 + t4
                    li = m * 4 + t4
                    j = li % 2
                    pm, pt = divmod(li - 2, 4)
                    lw = [("dve", evac.get((pm, pt), 0))]
                    df1 = c.dma("sync", f1t[j], b.F1[m, :, tti * 128:(tti + 1) * 128],
                                waits=tuple(lw) + (b.ffn_done if hasattr(b, "ffn_done") else ()))
                    dy0 = c.dma("sync", y0t[j], b.Y0fm[m, :, tti * 128:(tti + 1) * 128],
                                waits=tuple(lw))
                    for hh in range(NCH):
                        c.mm(pwo[j][:, 0:128], wosl[m % 2][:, hh * 128:(hh + 1) * 128],
                             ctg[:, hh * 512 + t4 * 128: hh * 512 + t4 * 128 + 128],
                             start=(hh == 0), stop=(hh == NCH - 1),
                             waits=(("pdma", dwo), ("sdma", dctg),
                                    ("dve", evac.get(("e", pm, pt), 0))),
                             inc="pe" if hh == NCH - 1 else None)
                    evac[("e", m, t4)] = c.cp("dve", ysb, pwo[j][:, 0:128],
                                              waits=c.w(("pe", None), ("act", None)), inc="dve")
                    c.tt("dve", tmp2, ysb, f1t[j], ALU.add, waits=(("sdma", df1),), inc="dve")
                    pm1, pt1 = divmod(li - 1, 4)
                    evac[(m, t4)] = c.tt("dve", y2, tmp2, y0t[j], ALU.add,
                                         waits=(("sdma", dy0), ("pe", evac.get(("tr", pm1, pt1), 0))),
                                         inc="dve")
                    evac[("tr", m, t4)] = c.tr(pT[:, 0:128], y2, idf,
                         waits=c.w(("dve", None)) + (("dve", evac.get(("t", pm, pt), 0)),),
                         inc="pe")
                    evac[("t", m, t4)] = c.cp("dve", otm[t4][:, m * 128:(m + 1) * 128],
                                              pT[:, 0:128],
                                              waits=c.w(("pe", None)) + (("sdma", st_cnt),),
                                              inc="dve")
                slab_use[m] = c.c("pe")
            for t4 in range(4):
                tti = tg * 4 + t4
                st_cnt = c.dma("sync", b.out[tti * 128:(tti + 1) * 128, :], otm[t4],
                               waits=c.w(("dve", None)))
    if layer == 0:
        b.wo0_done = c.c("dve")
    else:
        b.out_stored = st_cnt


def emit_gk(b):
    c = b.c
    pw = phase_barrier(b)
    idf = b.t["idf"]
    sgk = carve_ap(b.wta, 0, BF16, 128, 512)
    sig = carve_ap(b.arena, 0, F32, 128, 512)
    glog = carve_ap(b.arena, 2048, F32, 128, 512)
    gr = carve_ap(b.arena, 4096, F32, 128, 32)
    GrT = carve_ap(b.arena, 8192, F32, 32, 2048)
    b.qdT = carve_ap(b.arena, 71680, BF16, 32, 2048)
    b.kdT = carve_ap(b.arena, 75776, BF16, 32, 2048)
    b.dcols = carve_ap(b.arena, 79872, F32, 64, 512)
    dgk = c.dma("pool", sgk, b.wt["w_gk1"][:, :], waits=pw)
    pG = b.pb[0]
    ev = 0
    for tt in range(NTT):
        for ci in range(NCH):
            c.mm(pG[:, 0:32], b.xn[:, ci * S + tt * 128: ci * S + tt * 128 + 128],
                 sgk[:, ci * 32:(ci + 1) * 32], start=(ci == 0), stop=(ci == NCH - 1),
                 waits=(("pdma", dgk), ("act", ev)) + pw, inc="pe" if ci == NCH - 1 else None)
        ev = c.act(sig[:, tt * 32:(tt + 1) * 32], pG[:, 0:32], AF.Sigmoid,
                   waits=c.w(("pe", None)), inc="act")
    c.act(glog, sig, AF.Ln, waits=c.w(("act", None)), inc="act")
    lnn = c.c("act")
    pC, pT2 = b.pb[1], b.pb[2]
    ev2 = 0
    ev3 = 0
    for tt in range(NTT):
        c.mm(pC[:, 0:32], b.t["trif"], glog[:, tt * 32:(tt + 1) * 32],
             waits=(("act", lnn), ("dve", ev2)), inc="pe")
        ev2 = c.cp("dve", gr, pC[:, 0:32], waits=c.w(("pe", None)) + (("pe", ev3),), inc="dve")
        c.tr(pT2[0:32, 0:128], gr, idf, waits=(("dve", ev2), ("dve", ev3 and 0 or 0)), inc="pe")
        ev3 = c.c("pe")
        ev2b = c.cp("dve", GrT[:, tt * 128:(tt + 1) * 128], pT2[0:32, 0:128],
                    waits=c.w(("pe", None)), inc="dve")
    grn = c.c("dve")
    c.act(b.qdT, GrT, AF.Exp, scale=1.0 / 16, waits=(("dve", grn),), inc="act")
    c.act(b.kdT, GrT, AF.Exp, scale=-1.0 / 16, bias=b.constk[0:32, 3:4], inc="act")
    expn = c.c("act")
    pD = b.pb[3]
    qd3 = b.qdT.rearrange("p (t k) -> p t k", k=128)
    dv4 = 0
    for g in range(NCH):
        for hh in range(2):
            c.mm(pD[0:64, 0:16], b.t["selab"][:, (2 * g + hh) * 64:(2 * g + hh + 1) * 64],
                 qd3[:, :, 127], waits=(("act", expn), ("dve", dv4)), inc="pe")
            dv4 = c.cp("dve", b.dcols[:, g * 32 + hh * 16: g * 32 + hh * 16 + 16],
                       pD[0:64, 0:16], waits=c.w(("pe", None)), inc="dve")
    b.gk_done = c.c("dve")


ATT1_GROUPS = NCH
ATT1_NO_CROSS = False
ATT1_PREFETCH = True


def emit_att1(b):
    c = b.c
    pw = phase_barrier(b)
    onesc = b.t["ones_col_f"]
    onesr = b.t["ones_row_f"]
    idb = b.t["idb"]
    at = carve_ap
    og = at(b.arena, 0, BF16, 128, 2048)
    qA = at(b.arena, 4096, BF16, 64, 2048)
    qB = at(b.arena, 8192, BF16, 64, 2048)
    kA = at(b.arena, 12288, BF16, 64, 2048)
    kB = at(b.arena, 16384, BF16, 64, 2048)
    sgA = at(b.arena, 20480, BF16, 64, 2048)
    sgB = at(b.arena, 24576, BF16, 64, 2048)
    vtm = at(b.arena, 28672, BF16, 128, 2048)
    sl_q = [at(b.arena, 32768 + i * 4096, BF16, 128, 2048) for i in range(2)]
    sl_k = [at(b.arena, 40960 + i * 4096, BF16, 128, 2048) for i in range(2)]
    sl_g = [at(b.arena, 49152 + i * 4096, BF16, 128, 2048) for i in range(2)]
    sl_v = [at(b.arena, 57344 + i * 4096, BF16, 128, 2048) for i in range(2)]
    tmpg = at(b.arena, 65536, BF16, 64, 512)
    q2A = at(b.arena, 65536, BF16, 64, 128)
    q2B = at(b.arena, 65792, BF16, 64, 128)
    k2A = at(b.arena, 66048, BF16, 64, 128)
    k2B = at(b.arena, 66304, BF16, 64, 128)
    k2AT = at(b.arena, 66560, BF16, 128, 64)
    k2BT = at(b.arena, 66688, BF16, 128, 64)
    aTA = at(b.arena, 66816, BF16, 128, 128)
    aTB = at(b.arena, 67072, BF16, 128, 128)
    MA = at(b.arena, 67328, BF16, 64, 64)
    MB = at(b.arena, 67456, BF16, 64, 64)
    mdA = at(b.arena, 67584, F32, 64, 64)
    mdB = at(b.arena, 67840, F32, 64, 64)
    poA = at(b.arena, 68096, F32, 64, 128)
    poB = at(b.arena, 68608, F32, 64, 128)
    oA = at(b.arena, 69120, F32, 64, 128)
    oB = at(b.arena, 69632, F32, 64, 128)
    sqo = at(b.arena, 70144, F32, 64, 128)
    vrA = at(b.arena, 70656, F32, 1, 128)
    vrB = at(b.arena, 71168, F32, 1, 128)
    shufb = at(b.arena, 68096, BF16, 64, 512)
    rt1 = at(b.arena, 69120, BF16, 64, 512)
    rt2 = at(b.arena, 70144, BF16, 64, 512)

    pS_A, pS_B, pPO, pU, pD4, pV = b.pb[0], b.pb[1], b.pb[2], b.pb[3], b.pb[4], b.pb[5]
    phT = b.pbh[0]
    XOR1 = [i ^ 1 for i in range(32)]

    slab_use = {}
    prev = dict(pe_proj=0, og_dma=0, v_ev=0, sA=0, sB=0, po_ev=0, u_rd=0, d_rd=0,
                q2free=0, po_cp=0, ktn=0, var_rd=0, m_up_A=0, m_up_B=0)

    for g in range(ATT1_GROUPS):
        sw = list(pw)
        if g >= 2:
            sw.append(("pe", slab_use[g - 2]))
        d_q = c.dma("pool", sl_q[g % 2], b.wt["w_q1"][g, :, :], waits=tuple(sw))
        d_k = c.dma("pool", sl_k[g % 2], b.wt["w_k1"][g, :, :], waits=tuple(sw))
        d_g = c.dma("pool", sl_g[g % 2], b.wt["w_g1"][g, :, :], waits=tuple(sw))
        d_v = c.dma("pool", sl_v[g % 2], b.wt["w_v1"][g, :, :], waits=tuple(sw))
        evp = 0
        for which, slab, dd in (("q", sl_q[g % 2], d_q), ("k", sl_k[g % 2], d_k),
                                ("gg", sl_g[g % 2], d_g)):
            for tg in range(NTG):
                for ci in range(NCH):
                    c.mm(pS_A[:, 0:512], slab[:, ci * 128:(ci + 1) * 128],
                         b.xn[:, ci * S + tg * 512: ci * S + tg * 512 + 512],
                         start=(ci == 0), stop=(ci == NCH - 1),
                         waits=(("pdma", dd), ("dve", prev["sA"]), ("act", evp)) + pw,
                         inc="pe" if ci == NCH - 1 else None)
                sl2 = slice(tg * 512, tg * 512 + 512)
                if which == "q":
                    c.cp("dve", qA[:, sl2], pS_A[0:64, 0:512], waits=c.w(("pe", None)), inc="dve")
                    prev["sA"] = c.cp("dve", qB[:, sl2], pS_A[64:128, 0:512], inc="dve")
                elif which == "k":
                    c.cp("dve", kA[:, sl2], pS_A[0:64, 0:512], waits=c.w(("pe", None)), inc="dve")
                    prev["sA"] = c.cp("dve", kB[:, sl2], pS_A[64:128, 0:512], inc="dve")
                else:
                    c.act(sgA[:, sl2], pS_A[0:64, 0:512], AF.Silu,
                          waits=c.w(("pe", None)), inc="act")
                    gcp = c.cp("dve", tmpg, pS_A[64:128, 0:512],
                               waits=c.w(("pe", None), ("act", None)), inc="dve")
                    evp = c.act(sgB[:, sl2], tmpg, AF.Silu, waits=(("dve", gcp),), inc="act")
                    prev["sA"] = c.c("dve")
        for tt in range(NTT):
            for ci in range(NCH):
                c.mm(pV[:, 0:128], b.xn[:, ci * S + tt * 128: ci * S + tt * 128 + 128],
                     sl_v[g % 2][:, ci * 128:(ci + 1) * 128],
                     start=(ci == 0), stop=(ci == NCH - 1),
                     waits=(("pdma", d_v), ("dve", prev["v_ev"])),
                     inc="pe" if ci == NCH - 1 else None)
            prev["v_ev"] = c.cp("dve", vtm[:, tt * 128:(tt + 1) * 128], pV[:, 0:128],
                                waits=c.w(("pe", None)), inc="dve")
        for buf in (qA, qB, kA, kB):
            for tg in range(NTG):
                sl2 = slice(tg * 512, tg * 512 + 512)
                c.shuf(shufb, buf[:, sl2], XOR1, waits=c.w(("pe", None), ("act", None)), inc="dve")
                c.tt("dve", rt1, buf[:, sl2], b.t["l1cos"][0:64, tg * 512:(tg + 1) * 512],
                     ALU.mult, inc="dve")
                c.tt("dve", rt2, shufb, b.t["l1ss"][0:64, tg * 512:(tg + 1) * 512],
                     ALU.mult, inc="dve")
                c.tt("dve", buf[:, sl2], rt1, rt2, ALU.add, inc="dve")
        rope_done = c.c("dve")
        for t in range(NTT):
            tsl = slice(t * 128, (t + 1) * 128)
            c.mm(pD4[0:64, 0:128], b.t["selab"][:, (2 * g) * 64:(2 * g + 1) * 64],
                 b.qdT[:, tsl], waits=(("dve", prev["d_rd"]), ("dve", rope_done),
                                       ("dve", prev["var_rd"])),
                 inc="pe")
            c.mm(pD4[0:64, 128:256], b.t["selab"][:, (2 * g + 1) * 64:(2 * g + 2) * 64],
                 b.qdT[:, tsl], inc="pe")
            c.mm(pD4[0:64, 256:384], b.t["selab"][:, (2 * g) * 64:(2 * g + 1) * 64],
                 b.kdT[:, tsl], inc="pe")
            c.mm(pD4[0:64, 384:512], b.t["selab"][:, (2 * g + 1) * 64:(2 * g + 2) * 64],
                 b.kdT[:, tsl], inc="pe")
            dn = c.c("pe")
            n1 = c.tt("dve", q2A, qA[:, tsl], pD4[0:64, 0:128], ALU.mult,
                      waits=(("pe", dn), ("pe", prev["q2free"])), inc="dve")
            c.tt("dve", q2B, qB[:, tsl], pD4[0:64, 128:256], ALU.mult, inc="dve")
            c.tt("dve", k2A, kA[:, tsl], pD4[0:64, 256:384], ALU.mult, inc="dve")
            prev["d_rd"] = c.tt("dve", k2B, kB[:, tsl], pD4[0:64, 384:512], ALU.mult, inc="dve")
            qk_n = c.c("dve")
            c.mm(pS_A[:, 0:128], k2A, q2A, waits=(("dve", qk_n), ("dve", prev["sA"])), inc="pe")
            c.mm(pS_B[:, 0:128], k2B, q2B, waits=(("dve", prev["sB"]),), inc="pe")
            sn = c.c("pe")
            prev["sA"] = c.tt("dve", aTA, pS_A[:, 0:128], b.t["l1maskb"], ALU.mult,
                              waits=(("pe", sn),), inc="dve")
            prev["sB"] = c.tt("dve", aTB, pS_B[:, 0:128], b.t["l1maskb"], ALU.mult, inc="dve")
            a_n = c.c("dve")
            c.tr(phT[:, 0:64], k2A, idb[0:64, 0:64], waits=(("dve", qk_n), ("dve", prev["ktn"])))
            c.tr(phT[:, 64:128], k2B, idb[0:64, 0:64], inc="pe")
            tn = c.c("pe")
            c.cp("dve", k2AT, phT[:, 0:64], waits=(("pe", tn),), inc="dve")
            ktn = c.cp("dve", k2BT, phT[:, 64:128], inc="dve")
            prev["ktn"] = ktn
            stopA = (t == 0) or ATT1_NO_CROSS
            c.mm(pPO[0:64, 0:128], vtm[:, t * 128: t * 128 + 64], aTA,
                 start=True, stop=stopA, waits=(("dve", a_n), ("act", prev["po_ev"]),
                                                ("dve", prev["po_cp"])), inc="pe")
            if t > 0 and not ATT1_NO_CROSS:
                c.mm(pPO[0:64, 0:128], MA, q2A, start=False, stop=True,
                     waits=(("dve", prev["m_up_A"]),), inc="pe")
            c.mm(pPO[0:64, 128:256], vtm[:, t * 128 + 64: t * 128 + 128], aTB,
                 start=True, stop=stopA, inc="pe")
            if t > 0 and not ATT1_NO_CROSS:
                c.mm(pPO[0:64, 128:256], MB, q2B, start=False, stop=True,
                     waits=(("dve", prev["m_up_B"]),), inc="pe")
            po_n = c.c("pe")
            c.mm(pU[0:64, 0:64], k2AT, vtm[:, t * 128: t * 128 + 64],
                 waits=(("dve", ktn), ("dve", prev["u_rd"])), inc="pe")
            c.mm(pU[0:64, 64:128], k2BT, vtm[:, t * 128 + 64: t * 128 + 128], inc="pe")
            un = c.c("pe")
            dA = b.dcols[:, g * 32 + t: g * 32 + t + 1]
            dB = b.dcols[:, g * 32 + 16 + t: g * 32 + 16 + t + 1]
            if t == 0:
                m1 = c.ts("dve", MA, pU[0:64, 0:64], dA, ALU.mult, waits=(("pe", un),), inc="dve")
                m2 = c.ts("dve", MB, pU[0:64, 64:128], dB, ALU.mult, inc="dve")
                prev["m_up_A"] = m1
                prev["m_up_B"] = m2
                prev["u_rd"] = m2
            else:
                w1 = c.ts("dve", mdA, MA, dA, ALU.mult, waits=(("pe", un), ("pe", po_n)), inc="dve")
                w2 = c.ts("dve", mdB, MB, dB, ALU.mult, inc="dve")
                m1 = c.stt("dve", MA, pU[0:64, 0:64], dA, mdA, ALU.mult, ALU.add,
                           waits=(("dve!", w1),), inc="dve")
                m2 = c.stt("dve", MB, pU[0:64, 64:128], dB, mdB, ALU.mult, ALU.add,
                           waits=(("dve!", w2),), inc="dve")
                prev["m_up_A"] = m1
                prev["m_up_B"] = m2
                prev["u_rd"] = m2
            c.act(sqo, pPO[0:64, 0:128], AF.Square, waits=(("pe", po_n),), inc="act")
            c.mm(pD4[0:1, 0:128], onesc[0:64, 0:1], sqo, waits=c.w(("act", None), ("dve", qk_n)), inc="pe")
            c.act(sqo, pPO[0:64, 128:256], AF.Square, waits=c.w(("pe", None)), inc="act")
            c.mm(pD4[0:1, 128:256], onesc[0:64, 0:1], sqo, waits=c.w(("act", None)), inc="pe")
            c.act(vrA, pD4[0:1, 0:128], AF.Ln, bias=b.constk[0:1, 1:2], scale=1.0 / 64,
                  waits=c.w(("pe", None)), inc="act")
            c.act(vrA, vrA, AF.Exp, scale=-0.5, inc="act")
            c.act(vrB, pD4[0:1, 128:256], AF.Ln, bias=b.constk[0:1, 1:2], scale=1.0 / 64, inc="act")
            c.act(vrB, vrB, AF.Exp, scale=-0.5, inc="act")
            c.mm(pD4[0:64, 256:384], onesr[0:1, 0:64], vrA, waits=c.w(("act", None)), inc="pe")
            c.mm(pD4[0:64, 384:512], onesr[0:1, 0:64], vrB, inc="pe")
            bn = c.c("pe")
            c.cp("dve", poA, pPO[0:64, 0:128], waits=(("pe", bn),), inc="dve")
            pon = c.cp("dve", poB, pPO[0:64, 128:256], inc="dve")
            prev["po_cp"] = pon
            prev["po_ev"] = c.c("act")
            c.tt("dve", oA, poA, pD4[0:64, 256:384], ALU.mult, waits=(("dve!", pon),), inc="dve")
            c.tt("dve", oB, poB, pD4[0:64, 384:512], ALU.mult, inc="dve")
            prev["var_rd"] = c.c("dve")
            c.tt("dve", og[0:64, tsl], oA, sgA[:, tsl], ALU.mult,
                 waits=(("sdma", prev["og_dma"]),), inc="dve")
            c.tt("dve", og[64:128, tsl], oB, sgB[:, tsl], ALU.mult, inc="dve")
            prev["q2free"] = un
            if g == 0 and t == 15:
                snap6 = carve_ap(b.wta, 3840, BF16, 64, 128)
                c.cp("dve", snap6, og[0:64, 128:256], inc="dve")
                b.att1_snap6 = snap6
            if g == 0 and t == 1:
                snap = carve_ap(b.wta, 2048, BF16, 128, 128)
                snap2 = carve_ap(b.wta, 2304, BF16, 64, 128)
                snap3 = carve_ap(b.wta, 2560, F32, 64, 128)
                snap4 = carve_ap(b.wta, 3072, F32, 64, 128)
                snap5 = carve_ap(b.wta, 3584, BF16, 64, 128)
                c.cp("dve", snap, aTA, inc="dve")
                c.cp("dve", snap2, q2A, inc="dve")
                c.cp("dve", snap3, poA, inc="dve")
                c.cp("dve", snap4, oA, inc="dve")
                c.cp("dve", snap5, og[0:64, tsl], inc="dve")
                b.att1_snaps = {"snA": snap, "snQ2": snap2, "snPO": snap3,
                                "snO": snap4, "snOG": snap5}
        slab_use[g] = c.c("pe")
        prev["og_dma"] = c.dma("sync", b.O1[g, :, :], og, waits=c.w(("dve", None)))
    b.att1_done = c.w(("sdma", None))
    b.att1_taps = {"a1qA": qA, "a1qB": qB, "a1kA": kA, "a1vtm": vtm, "a1sgA": sgA,
                   "a1qdT": b.qdT, "a1kdT": b.kdT, "a1dcols": b.dcols, "a1og": og, "a1sgB": sgB,
                   "a1q2A": q2A, "a1k2A": k2A, "a1aTA": aTA}


def emit_debug(b):
    c = b.c
    if not b.debug:
        return
    allw = c.w(("dve", None), ("pe", None), ("act", None), ("pool", None),
               ("sdma", None))
    nc = b.nc

    def dump(name, shape, src_ap, queue="pool"):
        if name not in b.debug:
            return
        t = nc.dram_tensor("dbg_" + name, list(shape), F32, kind="ExternalOutput").ap()
        b.dbg_tensors[name] = t
        idx = tuple([slice(None)] * len(shape))
        c.dma(queue, t[idx], src_ap, waits=allw)

    dump("xn", [128, NCH * S], b.xn)
    dump("entr", [128, 2048], b.small["entr"])
    dump("ck", [128, 128], b.small["ck"])
    dump("cv", [128, 128], b.small["cv"])
    if "snOG2" in b.debug and hasattr(b, "att1_snap6"):
        ap = b.att1_snap6
        t = nc.dram_tensor("dbg_snOG2", [64, 128], F32, kind="ExternalOutput").ap()
        b.dbg_tensors["snOG2"] = t
        c.dma("pool", t[:, :], ap, waits=allw)
    for nm in ("snA", "snQ2", "snPO", "snO", "snOG"):
        if nm in b.debug and hasattr(b, "att1_snaps"):
            ap = b.att1_snaps[nm]
            t = nc.dram_tensor("dbg_" + nm, [ap.shape[0], ap.shape[1]], F32, kind="ExternalOutput").ap()
            b.dbg_tensors[nm] = t
            c.dma("pool", t[:, :], ap, waits=allw)
    for nm in ("a1qA", "a1qB", "a1kA", "a1vtm", "a1sgA", "a1qdT", "a1kdT", "a1dcols", "a1og", "a1sgB",
               "a1q2A", "a1k2A", "a1aTA"):
        if nm in b.debug and hasattr(b, "att1_taps"):
            ap = b.att1_taps[nm]
            t = nc.dram_tensor("dbg_" + nm, [ap.shape[0], ap.shape[1]], F32, kind="ExternalOutput").ap()
            b.dbg_tensors[nm] = t
            c.dma("pool", t[:, :], ap, waits=allw)
    for nm in ("qh", "lkb", "lvt", "scl", "eln", "atl", "ecn", "atc", "scc"):
        if nm in b.debug and hasattr(b, "att0_taps"):
            ap = b.att0_taps[nm]
            shp = [ap.shape[0], ap.shape[1]]
            t = nc.dram_tensor("dbg_" + nm, shp, F32, kind="ExternalOutput").ap()
            b.dbg_tensors[nm] = t
            c.dma("pool", t[:, :], ap, waits=allw)
    if "o1" in b.debug:
        t = nc.dram_tensor("dbg_o1", [NCH, 128, S], F32, kind="ExternalOutput").ap()
        b.dbg_tensors["o1"] = t
        c.dma("pool", t[:, :, :], b.O1[:, :, :], waits=allw)
    if "y0" in b.debug:
        t = nc.dram_tensor("dbg_y0", [NCH, 128, S], F32, kind="ExternalOutput").ap()
        b.dbg_tensors["y0"] = t
        c.dma("sync", t[:, :, :], b.Y0fm[:, :, :], waits=allw)
    if "ctx0" in b.debug:
        t = nc.dram_tensor("dbg_ctx0", [NH0, 128, S], F32, kind="ExternalOutput").ap()
        b.dbg_tensors["ctx0"] = t
        c.dma("pool", t[:, :, :], b.CTX0[:, :, :], waits=allw)
    for nm, tsr in [("f0", "F0"), ("f1", "F1")]:
        if nm in b.debug:
            t = nc.dram_tensor("dbg_" + nm, [NCH, 128, S], F32, kind="ExternalOutput").ap()
            b.dbg_tensors[nm] = t
            c.dma("sync", t[:, :, :], getattr(b, tsr)[:, :, :], waits=allw)
    if "x0fm" in b.debug:
        t = nc.dram_tensor("dbg_x0fm", [NCH, 128, S], F32, kind="ExternalOutput").ap()
        b.dbg_tensors["x0fm"] = t
        c.dma("sync", t[:, :, :], b.X0fm[:, :, :], waits=allw)


def finalize(b):
    c = b.c
    c.op("sync", lambda e: e.nop(),
         waits=c.w(("sdma", None), ("pdma", None), ("pe", None), ("dve", None),
                   ("act", None), ("pool", None)))



ALL_PHASES = ("ln0", "cmp", "ffn0", "att0", "wo0", "gk", "ffn1", "att1", "wo1")


def run_built(nc, b, inputs, cores=8):
    g = prep_inputs(inputs)
    consts = b.hconsts
    x = np.ascontiguousarray(inputs["L0_attn_hidden_states"].astype(np.float32, copy=False))
    in_maps = []
    for core in range(cores):
        m = {"x": x[core]}
        m.update(g)
        m.update({"c_" + k: v for k, v in consts.items()})
        in_maps.append(m)
    return run_bass_kernel_spmd(nc, in_maps, core_ids=list(range(cores)))


def kernel(**inputs):
    nc, b = build_program(ALL_PHASES)
    res = run_built(nc, b, inputs)
    return np.stack([np.asarray(r["OUT"], np.float32) for r in res.results], 0)


# version 11
# speedup vs baseline: 1.2682x; 1.2682x over previous
import numpy as np
import ml_dtypes
from contextlib import ExitStack

import concourse.bass as bass
import concourse.mybir as mybir
from concourse.bass_utils import run_bass_kernel_spmd

F32 = mybir.dt.float32
BF16 = mybir.dt.bfloat16
I8 = mybir.dt.int8
AF = mybir.ActivationFunctionType
ALU = mybir.AluOpType
AX = mybir.AxisListType
BF = ml_dtypes.bfloat16

S = 2048
DM = 2048
NCH = 16
TG = 512
NTG = S // TG
NTT = 16
NEG = -1.0e9

NH0, HD0, CBLK, RBLK = 16, 128, 128, 16
I0CH, Q0N = 64, 4
I1CH, Q1N = 44, 4
LN128 = float(np.log(128.0))

BARRIER_MODE = True
ATT0_NOBAR = True
BARRIER_WAITS = {"pe": ("dve", "act"), "sync": (), "pool": (),
                 "dve": ("pe", "act", "pool"), "act": ("pe", "dve", "pool")}
ENGINES = ("sync", "pool", "dve", "act", "pe")
BLOCK_ATTR = {"sync": "sync", "pool": "gpsimd", "dve": "vector", "act": "scalar", "pe": "tensor"}
SEMS = ("sdma", "pdma", "pe", "dve", "act", "pool")
DMA_SEM = {"sync": "sdma", "pool": "pdma"}



def tile_w(W):
    In, Out = W.shape
    nci = In // 128
    if Out < 128:
        return np.ascontiguousarray(
            W.reshape(nci, 128, 1, Out).transpose(2, 1, 0, 3).reshape(1, 128, nci * Out))
    nco = Out // 128
    return np.ascontiguousarray(
        W.reshape(nci, 128, nco, 128).transpose(2, 1, 0, 3).reshape(nco, 128, nci * 128))


def col_form(v):
    return np.ascontiguousarray(v.reshape(-1, 128).T)


def make_consts():
    c = {}
    c["idf"] = np.eye(128, dtype=np.float32)
    c["idb"] = np.eye(128).astype(BF)
    c["trif"] = np.triu(np.ones((128, 128))).astype(np.float32)
    c["l1maskb"] = np.triu(np.ones((128, 128))).astype(BF)
    c["ones_col_f"] = np.ones((128, 1), dtype=np.float32)
    c["ones_row_f"] = np.ones((1, 128), dtype=np.float32)
    c["onesblk"] = np.stack([np.arange(128) < 64, np.arange(128) >= 64], 1).astype(np.float32)
    inv = 10000.0 ** (-np.arange(32) / 32.0)
    t = np.arange(S)
    ang = t[None, :] * inv[:, None]
    c["l0cos"] = np.concatenate([np.cos(ang)] * 2, 0).astype(BF)
    c["l0ss"] = np.concatenate([-np.sin(ang), np.sin(ang)], 0).astype(BF)
    bend = np.arange(CBLK) * RBLK + (RBLK - 1)
    angc = bend[None, :] * inv[:, None]
    c["ckcos"] = np.concatenate([np.cos(angc)] * 2, 0).astype(BF)
    c["ckss"] = np.concatenate([-np.sin(angc), np.sin(angc)], 0).astype(BF)
    p = np.arange(64)
    invp = 10000.0 ** (-(p // 2) / 32.0)
    angl = t[None, :] * invp[:, None]
    sgn = np.where(p % 2 == 0, -1.0, 1.0)[:, None]
    c["l1cos"] = np.concatenate([np.cos(angl)] * 2, 0).astype(BF)
    c["l1ss"] = np.concatenate([sgn * np.sin(angl)] * 2, 0).astype(BF)
    i = np.arange(128)[:, None]
    j = np.arange(256)[None, :]
    band = (j >= i) & (j <= i + 128)
    c["mask_loc"] = np.where(band, 0.0, NEG).astype(np.float32)
    c["mask_loc0"] = np.where(band & (j >= 128), 0.0, NEG).astype(np.float32)
    jj = np.arange(264)[None, :]
    c["cmaskb"] = np.where(16 * jj <= 2033 + i, 0.0, NEG).astype(np.float32)
    sel = np.zeros((32, 2048), dtype=np.float32)
    for hh in range(32):
        sel[hh, hh * 64:(hh + 1) * 64] = 1.0
    c["selab"] = sel.astype(BF)
    return c


def prep_inputs(inp):
    g = {}
    f32 = np.float32
    g["w_q0"] = tile_w(inp["L0_attn_Wq"].astype(f32, copy=False))
    g["w_lk0"] = tile_w(inp["L0_attn_Wlk"].astype(f32, copy=False))
    g["w_lv0"] = tile_w(inp["L0_attn_Wlv"].astype(f32, copy=False))
    g["w_o0"] = tile_w(inp["L0_attn_Wo"].astype(f32, copy=False))
    g["w_k0"] = tile_w(inp["L0_attn_Wk"].astype(f32, copy=False))[0]
    g["w_v0"] = tile_w(inp["L0_attn_Wv"].astype(f32, copy=False))[0]
    g["w_c0"] = np.ascontiguousarray(inp["L0_attn_Wc"].reshape(16, 128).T)
    g["b_q0"] = col_form(inp["L0_attn_bq"])
    g["b_lk0"] = col_form(inp["L0_attn_blk"])
    g["b_o0"] = col_form(inp["L0_attn_bo"])
    g["b_lv0r"] = np.ascontiguousarray(inp["L0_attn_blv"].reshape(1, 2048))
    g["b_k0"] = np.ascontiguousarray(inp["L0_attn_bk"].reshape(128, 1))
    g["b_v0r"] = np.ascontiguousarray(inp["L0_attn_bv"].reshape(1, 128))
    g["b_c0"] = np.ascontiguousarray(inp["L0_attn_bc"].reshape(1, 1))
    g["qn_w"] = np.ascontiguousarray(inp["L0_attn_qn_w"].reshape(128, 1))
    g["kn_w"] = np.ascontiguousarray(inp["L0_attn_kn_w"].reshape(128, 1))
    g["sink_k"] = np.ascontiguousarray(inp["L0_attn_sink_k"].T)
    g["sink_v"] = np.ascontiguousarray(inp["L0_attn_sink_v"].reshape(1, 2048))
    g["w_g0"] = tile_w(inp["L0_ffn_w_gate"])
    g["w_u0"] = tile_w(inp["L0_ffn_w_up"])
    g["w_d0"] = tile_w(inp["L0_ffn_w_down"])
    g["s_g0"] = col_form(inp["L0_ffn_s_gate"])
    g["s_u0"] = col_form(inp["L0_ffn_s_up"])
    g["s_d0"] = col_form(inp["L0_ffn_s_down"])
    g["w_q1"] = tile_w(np.ascontiguousarray(inp["L1_attn_Wq"].T))
    g["w_k1"] = tile_w(np.ascontiguousarray(inp["L1_attn_Wk"].T))
    g["w_v1"] = tile_w(np.ascontiguousarray(inp["L1_attn_Wv"].T))
    g["w_g1"] = tile_w(np.ascontiguousarray(inp["L1_attn_Wg"].T))
    g["w_gk1"] = tile_w(np.ascontiguousarray(inp["L1_attn_Wgk"].T))[0]
    g["w_o1"] = tile_w(np.ascontiguousarray(inp["L1_attn_Wo"].T))
    g["w_g1f"] = tile_w(np.ascontiguousarray(inp["L1_ffn_gate_w"].T))
    g["w_u1f"] = tile_w(np.ascontiguousarray(inp["L1_ffn_up_w"].T))
    g["w_d1f"] = tile_w(np.ascontiguousarray(inp["L1_ffn_down_w"].T))
    g["s_g1"] = col_form(inp["L1_ffn_gate_s"])
    g["s_u1"] = col_form(inp["L1_ffn_up_s"])
    g["s_d1"] = col_form(inp["L1_ffn_down_s"])
    return g


WSPECS = {
    "w_q0": ([16, 128, 2048], F32), "w_lk0": ([16, 128, 2048], F32),
    "w_lv0": ([16, 128, 2048], F32), "w_o0": ([16, 128, 2048], F32),
    "w_k0": ([128, 2048], F32), "w_v0": ([128, 2048], F32), "w_c0": ([128, 16], F32),
    "b_q0": ([128, 16], F32), "b_lk0": ([128, 16], F32), "b_o0": ([128, 16], F32),
    "b_lv0r": ([1, 2048], F32), "b_k0": ([128, 1], F32), "b_v0r": ([1, 128], F32),
    "b_c0": ([1, 1], F32), "qn_w": ([128, 1], F32), "kn_w": ([128, 1], F32),
    "sink_k": ([128, 16], F32), "sink_v": ([1, 2048], F32),
    "w_g0": ([64, 128, 2048], I8), "w_u0": ([64, 128, 2048], I8),
    "w_d0": ([16, 128, 8192], I8),
    "s_g0": ([128, 64], F32), "s_u0": ([128, 64], F32), "s_d0": ([128, 16], F32),
    "w_q1": ([16, 128, 2048], F32), "w_k1": ([16, 128, 2048], F32),
    "w_v1": ([16, 128, 2048], F32), "w_g1": ([16, 128, 2048], F32),
    "w_gk1": ([128, 512], F32), "w_o1": ([16, 128, 2048], F32),
    "w_g1f": ([44, 128, 2048], F32), "w_u1f": ([44, 128, 2048], F32),
    "w_d1f": ([16, 128, 5632], F32),
    "s_g1": ([128, 44], F32), "s_u1": ([128, 44], F32), "s_d1": ([128, 16], F32),
}



class Ctx:
    def __init__(self):
        self.ops = {e: [] for e in ENGINES}
        self.cnt = {s: 0 for s in SEMS}

    def op(self, eng, emit, waits=(), inc=None, amt=1, nobar=False):
        if BARRIER_MODE and inc is None and eng in ("dve", "act", "pool", "pe"):
            inc = {"dve": "dve", "act": "act", "pool": "pool", "pe": "pe"}[eng]
        elif eng == "dve" and inc is None:
            inc = "dve"
        for sname, v in waits:
            assert v <= self.cnt[sname.rstrip("!")], \
                f"wait on future: {eng} {sname} {v} > {self.cnt[sname.rstrip('!')]}"
        snap = dict(self.cnt) if (BARRIER_MODE and not nobar) else None
        if inc is not None:
            self.cnt[inc] += amt
        self.ops[eng].append((emit, tuple(waits), inc, amt, snap))
        return self.cnt[inc] if inc is not None else None

    def dma(self, q, out, in_, waits=(), **kw):
        sem = DMA_SEM[q]
        return self.op(q, lambda e, o=out, i=in_, k=kw: e.dma_start(out=o, in_=i, **k),
                       waits, sem, 16)

    def mm(self, out, lhsT, rhs, start=True, stop=True, waits=(), inc=None, nobar=False, **kw):
        return self.op("pe", lambda e, o=out, l=lhsT, r=rhs, s1=start, s2=stop, k=kw:
                       e.matmul(o, l, r, start=s1, stop=s2, **k), waits, inc, nobar=nobar)

    def tr(self, out, in_, ident, waits=(), inc=None, nobar=False):
        return self.op("pe", lambda e, o=out, i=in_, d=ident: e.transpose(o, i, d), waits, inc,
                       nobar=nobar)

    def cp(self, eng, out, in_, waits=(), inc=None):
        return self.op(eng, lambda e, o=out, i=in_: e.tensor_copy(o, i), waits, inc)

    def tt(self, eng, out, in0, in1, op, waits=(), inc=None):
        return self.op(eng, lambda e, o=out, a=in0, b=in1, p=op: e.tensor_tensor(o, a, b, p),
                       waits, inc)

    def ts(self, eng, out, in0, s1, op0, s2=None, op1=ALU.mult, waits=(), inc=None):
        def emit(e, o=out, a=in0, x=s1, y=s2, p0=op0, p1=op1):
            if y is None:
                return e.tensor_scalar(o, a, x, None, p0)
            return e.tensor_scalar(o, a, x, y, p0, p1)
        return self.op(eng, emit, waits, inc)

    def stt(self, eng, out, in0, scalar, in1, op0, op1, waits=(), inc=None):
        return self.op(eng, lambda e, o=out, a=in0, s=scalar, b=in1, p0=op0, p1=op1:
                       e.scalar_tensor_tensor(o, a, s, b, p0, p1), waits, inc)

    def act(self, out, in_, func, bias=0.0, scale=1.0, accum_out=None, waits=(), inc=None):
        return self.op("act", lambda e, o=out, i=in_, f=func, b=bias, s=scale, ao=accum_out:
                       e.activation(o, i, f, bias=b, scale=s, accum_out=ao), waits, inc)

    def red(self, eng, out, in_, op, negate=False, waits=(), inc=None):
        return self.op(eng, lambda e, o=out, i=in_, p=op, n=negate:
                       e.tensor_reduce(o, i, AX.X, p, negate=n), waits, inc)

    def memset(self, eng, ap, val, waits=(), inc=None):
        return self.op(eng, lambda e, a=ap, v=val: e.memset(a, v), waits, inc)

    def shuf(self, out, in_, mask, waits=(), inc=None):
        return self.op("dve", lambda e, o=out, i=in_, m=mask: e.stream_shuffle(o, i, list(m)),
                       waits, inc)

    def recip(self, out, in_, waits=(), inc=None):
        return self.op("dve", lambda e, o=out, i=in_: e.reciprocal(o, i), waits, inc)

    def c(self, sem):
        return self.cnt[sem]

    def w(self, *pairs):
        return tuple((s, self.cnt[s] if v is None else v) for s, v in pairs)


def emit_block(nc, ctx, es):
    block = es.enter_context(nc.Block())
    sems = {s: es.enter_context(nc.semaphore(s)) for s in SEMS}
    mile = {"sync": None, "pool": "pool", "dve": "dve", "act": "act", "pe": "pe"}
    for ename in ENGINES:
        section = getattr(block, BLOCK_ATTR[ename])

        def body(engobj, _e=ename):
            last = {}
            prog = {s: 0 for s in SEMS}
            own = mile[_e]
            for emit, waits, inc, amt, snap in ctx.ops[_e]:
                if _e == "dve" and prog["dve"] > 0:
                    engobj.wait_ge(sems["dve"], prog["dve"])
                    last["dve"] = max(last.get("dve", -1), prog["dve"])
                if snap is not None:
                    for sname2 in BARRIER_WAITS[_e]:
                        v2 = snap[sname2]
                        if v2 > 0 and last.get(sname2, -1) < v2:
                            engobj.wait_ge(sems[sname2], v2)
                            last[sname2] = v2
                for sname, v in waits:
                    force = sname.endswith("!")
                    if force:
                        sname = sname[:-1]
                    if sname == own and not force:
                        assert v <= prog[sname], f"{_e} waits own future {sname} {v} > {prog[sname]}"
                        continue
                    if not force and last.get(sname, -1) >= v:
                        continue
                    engobj.wait_ge(sems[sname], v)
                    last[sname] = max(last.get(sname, -1), v)
                inst = emit(engobj)
                if inc is not None:
                    assert inst is not None, f"{_e} inc without instruction"
                    inst.then_inc(sems[inc], amt)
                    prog[inc] += amt

        section(body)
    return sems



class B:
    def __init__(self, nc, ctx, debug=()):
        self.nc = nc
        self.c = ctx
        self.debug = set(debug)
        self.dbg_tensors = {}


def carve_ap(base_ap, off_bytes, dtype, p, f):
    esz = {F32: 4, BF16: 2, I8: 1}[dtype]
    assert off_bytes % esz == 0
    v = base_ap.bitcast(dtype)
    return v[0:p, off_bytes // esz: off_bytes // esz + f]


def build_program(phases, debug=()):
    nc = bass.Bass("TRN2", target_bir_lowering=False)
    ctx = Ctx()
    b = B(nc, ctx, debug)
    b.phases = set(phases)

    dt = nc.dram_tensor
    b.x = dt("x", [S, DM], F32, kind="ExternalInput").ap()
    b.out = dt("OUT", [S, DM], F32, kind="ExternalOutput").ap()
    b.X0fm = dt("X0fm", [NCH, 128, S], F32, kind="Internal").ap()
    b.F0 = dt("F0", [NCH, 128, S], F32, kind="Internal").ap()
    b.Y0fm = dt("Y0fm", [NCH, 128, S], F32, kind="Internal").ap()
    b.F1 = dt("F1", [NCH, 128, S], F32, kind="Internal").ap()
    b.CTX0 = dt("CTX0", [NH0, 128, S], BF16, kind="Internal").ap()
    b.O1 = dt("O1", [NCH, 128, S], BF16, kind="Internal").ap()

    b.hconsts = make_consts()
    b.tin = {}
    for k, v in b.hconsts.items():
        dty = BF16 if v.dtype == BF else F32
        b.tin[k] = dt("c_" + k, list(v.shape), dty, kind="ExternalInput").ap()
    b.wt = {k: dt(k, list(shp), dty, kind="ExternalInput").ap() for k, (shp, dty) in WSPECS.items()}

    es = ExitStack()
    with es:
        sbt = lambda n, shp, dty: es.enter_context(nc.sbuf_tensor(n, shp, dty))
        b.xn = sbt("xn", [128, NCH * S], BF16)[:]
        b.arena = sbt("arena", [128, 81920], I8)[:]
        b.wta = sbt("wta", [128, 16384], I8)[:]
        b.scr = sbt("scr", [128, 4096], I8)[:]
        b.t = {}
        for k, v in b.hconsts.items():
            dty = BF16 if v.dtype == BF else F32
            b.t[k] = sbt("s_" + k, list(v.shape), dty)[:]
        SMALL = {
            "wc_b": ([128, 16], BF16), "bq": ([128, 16], F32), "blk": ([128, 16], F32),
            "bo": ([128, 16], F32), "bk": ([128, 1], F32),
            "bvr": ([1, 128], F32), "bc": ([1, 1], F32), "qn": ([128, 1], F32),
            "kn": ([128, 1], F32), "sinkk": ([128, 16], BF16),
            "blv_h": ([1, 128], F32), "sv_h": ([1, 128], BF16),
            "ck": ([128, 128], BF16), "cv": ([128, 128], BF16),
            "entr": ([128, 2048], BF16),
            "sg0": ([128, 64], F32), "su0": ([128, 64], F32), "sd0": ([128, 16], F32),
            "sg1": ([128, 44], F32), "su1": ([128, 44], F32), "sd1": ([128, 16], F32),
        }
        b.small = {k: sbt("sm_" + k, list(shp), dty)[:] for k, (shp, dty) in SMALL.items()}
        b.constk = sbt("constk", [128, 8], F32)[:]
        b.pb = [es.enter_context(nc.psum_tensor(f"pb{i}", [128, 512], F32))[:] for i in range(6)]
        b.pbh = [es.enter_context(nc.psum_tensor(f"pbh{i}", [128, 1024], BF16))[:] for i in range(2)]

        load_consts(b)
        if "ln0" in b.phases:
            emit_ln0(b)
        if "cmp" in b.phases:
            emit_cmp(b)
        if "ffn0" in b.phases:
            emit_ffn(b, 0)
        if "att0" in b.phases:
            emit_att0(b)
        if "wo0" in b.phases:
            emit_wo(b, 0)
        if "gk" in b.phases:
            emit_gk(b)
        if "ffn1" in b.phases:
            emit_ffn(b, 1)
        if "att1" in b.phases:
            emit_att1(b)
        if "wo1" in b.phases:
            emit_wo(b, 1)
        emit_debug(b)
        finalize(b)
        emit_block(nc, ctx, es)
    return nc, b


def phase_barrier(b):
    c = b.c
    b.pw = c.w(("sdma", None), ("pdma", None), ("pe", None), ("dve", None),
               ("act", None), ("pool", None))
    return b.pw


def load_consts(b):
    c = b.c
    for k in b.hconsts:
        c.dma("sync", b.t[k], b.tin[k][:, :])
    sm, wt = b.small, b.wt
    for dst, src in [("bq", "b_q0"), ("blk", "b_lk0"), ("bo", "b_o0"),
                     ("bk", "b_k0"), ("bvr", "b_v0r"), ("bc", "b_c0"), ("qn", "qn_w"),
                     ("kn", "kn_w"), ("sg0", "s_g0"), ("su0", "s_u0"), ("sd0", "s_d0"),
                     ("sg1", "s_g1"), ("su1", "s_u1"), ("sd1", "s_d1")]:
        c.dma("sync", sm[dst], wt[src][:, :])
    c.dma("pool", sm["wc_b"], wt["w_c0"][:, :])
    c.dma("pool", sm["sinkk"], wt["sink_k"][:, :])
    for col, val in [(0, 1e-5), (1, 1e-6), (2, -0.5 * LN128), (3, float(np.log(0.125)))]:
        c.memset("pool", b.constk[:, col:col + 1], val, inc="pool")
    b.consts_loaded = c.w(("sdma", None), ("pdma", None), ("pool", None))



def emit_ln0(b):
    c = b.c
    idf = b.t["idf"]
    onesc = b.t["ones_col_f"]
    onesr = b.t["ones_row_f"]
    X0g = carve_ap(b.arena, 0, F32, 128, 8192)
    xtm = [carve_ap(b.arena, 32768 + i * 8192, F32, 128, 2048) for i in range(2)]
    rows = carve_ap(b.arena, 49152, F32, 1, 1536)
    sq = carve_ap(b.scr, 0, F32, 128, 512)
    tmp = carve_ap(b.scr, 2048, F32, 128, 512)
    pT = [b.pb[0], b.pb[1]]
    pS, pS2, pM, pR = b.pb[2], b.pb[3], b.pb[4], b.pb[5]

    lw = b.consts_loaded
    pe_done = {}
    evac = []
    store_prev = 0
    for tg in range(NTG):
        for tt in range(4):
            tti = tg * 4 + tt
            buf = xtm[tti % 2]
            waits = list(lw)
            if tti >= 2:
                waits.append(("pe", pe_done[tti - 2]))
            dn = c.dma("sync", buf, b.x[tti * 128:(tti + 1) * 128, :], waits=tuple(waits))
            for ci in range(NCH):
                bank = pT[ci % 2]
                trw = [("sdma", dn)]
                if len(evac) >= 2:
                    trw.append(("dve", evac[-2]))
                c.tr(bank[:, 0:128], buf[:, ci * 128:(ci + 1) * 128], idf,
                     waits=tuple(trw), inc="pe")
                ev = c.cp("dve", X0g[:, ci * 512 + tt * 128: ci * 512 + tt * 128 + 128],
                          bank[:, 0:128],
                          waits=c.w(("pe", None)) + (("sdma", store_prev),), inc="dve")
                evac.append(ev)
            pe_done[tti] = c.c("pe")
        for ci in range(NCH):
            xsl = X0g[:, ci * 512:(ci + 1) * 512]
            c.act(sq, xsl, AF.Square, waits=c.w(("dve", None), ("pe", None)), inc="act")
            c.mm(pS[0:1, 0:512], onesc, xsl, start=(ci == 0), stop=(ci == NCH - 1),
                 waits=c.w(("dve", None), ("act", None)), inc="pe")
            c.mm(pS2[0:1, 0:512], onesc, sq, start=(ci == 0), stop=(ci == NCH - 1),
                 waits=c.w(("act", None)), inc="pe")
        sx = rows[0:1, 0:512]
        sx2 = rows[0:1, 512:1024]
        wk = rows[0:1, 1024:1536]
        c.ts("dve", sx, pS[0:1, 0:512], 1.0 / DM, ALU.mult, waits=c.w(("pe", None)), inc="dve")
        c.ts("dve", sx2, pS2[0:1, 0:512], 1.0 / DM, ALU.mult, inc="dve")
        c.tt("dve", wk, sx, sx, ALU.mult, inc="dve")
        c.tt("dve", sx2, sx2, wk, ALU.subtract, inc="dve")
        c.act(wk, sx2, AF.Ln, bias=b.constk[0:1, 0:1], scale=1.0,
              waits=c.w(("dve", None)), inc="act")
        c.act(wk, wk, AF.Exp, scale=-0.5, inc="act")
        c.mm(pM[:, 0:512], onesr, sx, waits=c.w(("act", None), ("dve", None)), inc="pe")
        c.mm(pR[:, 0:512], onesr, wk, inc="pe")
        mmn = c.c("pe")
        for ci in range(NCH):
            xsl = X0g[:, ci * 512:(ci + 1) * 512]
            c.tt("dve", tmp, xsl, pM[:, 0:512], ALU.subtract, waits=(("pe", mmn),), inc="dve")
            c.tt("dve", b.xn[:, ci * S + tg * 512: ci * S + tg * 512 + 512], tmp,
                 pR[:, 0:512], ALU.mult, inc="dve")
        store_prev = c.dma("sync",
                           b.X0fm[:, :, tg * 512:(tg + 1) * 512].rearrange("ch p t -> p ch t"),
                           X0g.rearrange("p (ch t) -> p ch t", ch=NCH),
                           waits=c.w(("dve", None)))
        store_prev = c.dma("sync",
                           b.F0[:, :, tg * 512:(tg + 1) * 512].rearrange("ch p t -> p ch t"),
                           X0g.rearrange("p (ch t) -> p ch t", ch=NCH))
    b.ln0_xn = c.c("dve")
    b.ln0_store = store_prev
    b.ln0_f0init = store_prev



def emit_cmp(b):
    c = b.c
    pw = phase_barrier(b)
    sm = b.small
    onesc = b.t["ones_col_f"]
    onesr = b.t["ones_row_f"]
    wb = carve_ap(b.arena, 0, BF16, 128, 2048)
    tmpf = carve_ap(b.arena, 8192, F32, 128, 2048)
    lg = carve_ap(b.arena, 16384, F32, 1, 2048)
    erow = carve_ap(b.arena, 24576, F32, 1, 2048)
    wrow = carve_ap(b.arena, 32768, F32, 1, 2048)
    mrow = carve_ap(b.arena, 40960, F32, 1, 128)
    srow = carve_ap(b.arena, 41472, F32, 1, 128)
    vrow = carve_ap(b.arena, 41984, F32, 1, 128)
    rrow = carve_ap(b.arena, 42496, F32, 1, 128)
    ckf = carve_ap(b.scr, 0, F32, 128, 128)
    sqc = carve_ap(b.scr, 512, F32, 128, 128)
    swp = carve_ap(b.scr, 1024, BF16, 64, 128)
    t1 = carve_ap(b.scr, 1280, BF16, 64, 128)
    t2 = carve_ap(b.scr, 1536, BF16, 64, 128)
    ered = carve_ap(b.scr, 2048, F32, 128, 128)

    xn_ready = b.ln0_xn
    lw = b.consts_loaded
    pL = b.pb[0]
    for tg in range(NTG):
        for ci in range(NCH):
            c.mm(pL[0:1, 0:512], sm["wc_b"][:, ci:ci + 1],
                 b.xn[:, ci * S + tg * 512: ci * S + tg * 512 + 512],
                 start=(ci == 0), stop=(ci == NCH - 1),
                 waits=(("dve", xn_ready),) + lw + c.w(("act", None)),
                 inc="pe" if ci == NCH - 1 else None)
        c.act(lg[0:1, tg * 512:(tg + 1) * 512], pL[0:1, 0:512], AF.Identity,
              bias=sm["bc"][0:1, 0:1], waits=pw + c.w(("pe", None)), inc="act")
    lg3 = lg.rearrange("p (blk r) -> p blk r", r=16)
    c.red("dve", mrow, lg3, ALU.max, negate=True, waits=c.w(("act", None)), inc="dve")
    mb = mrow.rearrange("p (blk one) -> p blk one", one=1).broadcast_to([1, 128, 16])
    er3 = erow.rearrange("p (blk r) -> p blk r", r=16)
    c.tt("dve", er3, lg3, mb, ALU.add, inc="dve")
    c.act(erow, erow, AF.Exp, waits=c.w(("dve", None)), inc="act")
    c.red("dve", srow, er3, ALU.add, waits=c.w(("act", None)), inc="dve")
    c.recip(srow, srow, inc="dve")
    sb3 = srow.rearrange("p (blk one) -> p blk one", one=1).broadcast_to([1, 128, 16])
    wr3 = wrow.rearrange("p (blk r) -> p blk r", r=16)
    c.tt("dve", wr3, er3, sb3, ALU.mult, inc="dve")
    wn = c.c("dve")
    for tg in range(NTG):
        c.mm(b.pb[1][:, 0:512], onesr, wrow[0:1, tg * 512:(tg + 1) * 512],
             waits=(("dve", wn),) + c.w(("dve", None)), inc="pe")
        c.cp("dve", wb[:, tg * 512:(tg + 1) * 512], b.pb[1][:, 0:512],
             waits=pw + c.w(("pe", None)), inc="dve")
    for ci in range(NCH):
        c.tt("dve", tmpf, b.xn[:, ci * S:(ci + 1) * S], wb, ALU.mult, waits=pw, inc="dve")
        c.red("dve", ered, tmpf.rearrange("p (blk r) -> p blk r", r=16), ALU.add, inc="dve")
        c.cp("dve", sm["entr"][:, ci * 128:(ci + 1) * 128], ered, inc="dve")
    entr_n = c.c("dve")
    wk_ = carve_ap(b.wta, 0, BF16, 128, 2048)
    wv_ = carve_ap(b.wta, 4096, BF16, 128, 2048)
    k1 = c.dma("pool", wk_, b.wt["w_k0"][:, :])
    k2 = c.dma("pool", wv_, b.wt["w_v0"][:, :])
    pCK = b.pb[2]
    for ci in range(NCH):
        c.mm(pCK[:, 0:128], wk_[:, ci * 128:(ci + 1) * 128], sm["entr"][:, ci * 128:(ci + 1) * 128],
             start=(ci == 0), stop=(ci == NCH - 1),
             waits=(("dve", entr_n), ("pdma", k1)), inc="pe" if ci == NCH - 1 else None)
    c.act(ckf, pCK[:, 0:128], AF.Identity, bias=sm["bk"][:, 0:1], waits=c.w(("pe", None)), inc="act")
    c.act(sqc, ckf, AF.Square, inc="act")
    c.mm(b.pb[3][0:1, 0:128], onesc, sqc, waits=c.w(("act", None)), inc="pe")
    c.act(vrow, b.pb[3][0:1, 0:128], AF.Ln, bias=b.constk[0:1, 1:2], scale=1.0 / 128,
          waits=c.w(("pe", None)), inc="act")
    c.act(rrow, vrow, AF.Exp, scale=-0.5, inc="act")
    c.mm(b.pb[3][:, 128:256], onesr, rrow, waits=c.w(("act", None)), inc="pe")
    c.stt("dve", sm["ck"], ckf, sm["kn"][:, 0:1], b.pb[3][:, 128:256], ALU.mult, ALU.mult,
          waits=c.w(("pe", None), ("act", None)), inc="dve")
    ck = sm["ck"]
    c.cp("dve", swp[0:32, :], ck[32:64, 0:128], inc="dve")
    c.cp("dve", swp[32:64, :], ck[0:32, 0:128], inc="dve")
    c.tt("dve", t1, ck[0:64, 0:128], b.t["ckcos"], ALU.mult, inc="dve")
    c.tt("dve", t2, swp, b.t["ckss"], ALU.mult, inc="dve")
    c.tt("dve", ck[0:64, 0:128], t1, t2, ALU.add, inc="dve")
    pCV = b.pb[4]
    for ci in range(NCH):
        c.mm(pCV[:, 0:128], sm["entr"][:, ci * 128:(ci + 1) * 128], wv_[:, ci * 128:(ci + 1) * 128],
             start=(ci == 0), stop=False, waits=(("dve", entr_n), ("pdma", k2)))
    c.mm(pCV[:, 0:128], onesr, sm["bvr"][0:1, 0:128], start=False, stop=True,
         waits=lw, inc="pe")
    c.cp("dve", sm["cv"], pCV[:, 0:128], waits=c.w(("pe", None)), inc="dve")
    b.cmp_done = c.c("dve")




FFN_NQ = None
FFN_SKIP_B = False


def emit_ffn(b, layer):
    c = b.c
    pw = phase_barrier(b)
    if layer == 0:
        nI, NQ = I0CH, 4
        wg, wu, wd = b.wt["w_g0"], b.wt["w_u0"], b.wt["w_d0"]
        sg, su, sd = b.small["sg0"], b.small["su0"], b.small["sd0"]
        F = b.F0
    else:
        nI, NQ = I1CH, 4
        wg, wu, wd = b.wt["w_g1f"], b.wt["w_u1f"], b.wt["w_d1f"]
        sg, su, sd = b.small["sg1"], b.small["su1"], b.small["sd1"]
        F = b.F1
    Q = nI // NQ
    if FFN_NQ is not None:
        NQ = FFN_NQ
    hq = carve_ap(b.arena, 0, BF16, 128, Q * 2048)
    slab0 = Q * 4096
    gsl = [carve_ap(b.arena, slab0 + i * 4096, BF16, 128, 2048) for i in range(2)]
    usl = [carve_ap(b.arena, slab0 + 8192 + i * 4096, BF16, 128, 2048) for i in range(2)]
    dsl = [carve_ap(b.arena, slab0 + i * 4096, BF16, 128, Q * 128) for i in range(2)]
    dout = [carve_ap(b.arena, slab0 + 8192 + i * 2048, F32, 128, 512) for i in range(2)]
    hg = carve_ap(b.scr, 0, BF16, 128, 512)
    uu = carve_ap(b.scr, 1024, BF16, 128, 512)

    pg_use = {}
    pu_use = {}
    pd_evac = {}
    slab_use = {}
    dsl_use = {}
    dout_dma = {}
    last_fstore = 0
    for qi in range(NQ):
        qw = c.w(("pe", None), ("dve", None), ("act", None)) + pw
        for mi in range(Q):
            m = qi * Q + mi
            sw = list(qw)
            if mi >= 2:
                sw.append(("pe", slab_use[mi - 2]))
            if qi > 0:
                sw.append(("pdma", last_fstore))
            g_d = c.dma("pool", gsl[mi % 2], wg[m, :, :], waits=tuple(sw))
            u_d = c.dma("pool", usl[mi % 2], wu[m, :, :], waits=tuple(sw))
            for tg in range(NTG):
                it = mi * NTG + tg
                bg = b.pb[it % 2]
                bu = b.pb[2 + it % 2]
                mmw = [("pdma", g_d)]
                if it >= 2:
                    mmw.append(("act", pg_use[it - 2]))
                for ci in range(NCH):
                    c.mm(bg[:, 0:512], gsl[mi % 2][:, ci * 128:(ci + 1) * 128],
                         b.xn[:, ci * S + tg * 512: ci * S + tg * 512 + 512],
                         start=(ci == 0), stop=(ci == NCH - 1),
                         waits=tuple(mmw) + qw, inc="pe" if ci == NCH - 1 else None)
                mmw2 = [("pdma", u_d)]
                if it >= 2:
                    mmw2.append(("dve", pu_use[it - 2]))
                for ci in range(NCH):
                    c.mm(bu[:, 0:512], usl[mi % 2][:, ci * 128:(ci + 1) * 128],
                         b.xn[:, ci * S + tg * 512: ci * S + tg * 512 + 512],
                         start=(ci == 0), stop=(ci == NCH - 1),
                         waits=tuple(mmw2) + qw, inc="pe" if ci == NCH - 1 else None)
                pen = c.c("pe")
                pg_use[it] = c.act(hg, bg[:, 0:512], AF.Silu, scale=sg[:, m:m + 1],
                                   waits=(("pe", pen),) + c.w(("dve", None)), inc="act")
                pu_use[it] = c.ts("dve", uu, bu[:, 0:512], su[:, m:m + 1], ALU.mult,
                                  waits=(("pe", pen),), inc="dve")
                c.tt("dve", hq[:, mi * 2048 + tg * 512: mi * 2048 + tg * 512 + 512],
                     hg, uu, ALU.mult, waits=c.w(("act", None)), inc="dve")
            slab_use[mi] = c.c("pe")
        hq_done = c.c("dve")
        if FFN_SKIP_B:
            continue
        bw = c.w(("dve", hq_done), ("act", None))
        for hid in range(NCH):
            sw = list(bw)
            if hid >= 2:
                sw.append(("pe", dsl_use[hid - 2]))
            d_d = c.dma("pool", dsl[hid % 2], wd[hid, :, qi * Q * 128:(qi + 1) * Q * 128],
                        waits=tuple(sw))
            for tg in range(NTG):
                jt = hid * NTG + tg
                bd = b.pb[4 + jt % 2]
                mmw = [("pdma", d_d)]
                if jt >= 2:
                    mmw.append(("dve", pd_evac[jt - 2]))
                for mi in range(Q):
                    c.mm(bd[:, 0:512], dsl[hid % 2][:, mi * 128:(mi + 1) * 128],
                         hq[:, mi * 2048 + tg * 512: mi * 2048 + tg * 512 + 512],
                         start=(mi == 0), stop=(mi == Q - 1),
                         waits=tuple(mmw) + bw, inc="pe" if mi == Q - 1 else None)
                dw = [("pe", c.c("pe"))]
                if jt >= 2:
                    dw.append(("pdma", dout_dma[jt - 2]))
                pd_evac[jt] = c.ts("dve", dout[jt % 2], bd[:, 0:512], sd[:, hid:hid + 1],
                                   ALU.mult, waits=tuple(dw), inc="dve")
                stw = [("dve", pd_evac[jt])]
                if qi > 0:
                    stw.append(("pdma", last_fstore))
                accum = (qi > 0) or (layer == 0)
                if layer == 0 and qi == 0 and hasattr(b, "ln0_f0init"):
                    stw.append(("sdma", b.ln0_f0init))
                dout_dma[jt] = c.dma("pool", F[hid, :, tg * 512:(tg + 1) * 512],
                                     dout[jt % 2],
                                     waits=tuple(stw),
                                     **({"accum_op": ALU.add} if accum else {}))
            dsl_use[hid] = c.c("pe")
        last_fstore = c.c("pdma")
    b.ffn_done = c.w(("pdma", None))




ATT0_HEADS = NH0


def emit_att0(b):
    c = b.c
    pw = phase_barrier(b)
    sm = b.small
    onesc = b.t["ones_col_f"]
    onesr = b.t["ones_row_f"]
    idb = b.t["idb"]
    qh = carve_ap(b.arena, 0, BF16, 128, 2048)
    lkb = carve_ap(b.arena, 4096, BF16, 128, 2176)
    lvt = carve_ap(b.arena, 8448, BF16, 128, 2176)
    ctxh = carve_ap(b.arena, 12800, BF16, 128, 2048)
    qsl = [carve_ap(b.arena, 16896 + i * 4096, BF16, 128, 2048) for i in range(2)]
    lksl = [carve_ap(b.arena, 25088 + i * 4096, BF16, 128, 2048) for i in range(2)]
    lvsl = [carve_ap(b.arena, 33280 + i * 4096, BF16, 128, 2048) for i in range(2)]
    scl = carve_ap(b.arena, 41472, F32, 128, 256)
    scc = carve_ap(b.arena, 42496, F32, 128, 132)
    el = carve_ap(b.arena, 43520, BF16, 128, 256)
    eln = carve_ap(b.arena, 44032, BF16, 128, 256)
    ec = carve_ap(b.arena, 44544, BF16, 128, 132)
    ecn = carve_ap(b.arena, 45056, BF16, 128, 132)
    atl = carve_ap(b.arena, 45568, BF16, 128, 256)
    atc = carve_ap(b.arena, 46080, BF16, 128, 128)
    srow = carve_ap(b.arena, 46336, BF16, 1, 128)
    qf32 = carve_ap(b.arena, 47104, F32, 128, 512)
    sqf = carve_ap(b.arena, 49152, F32, 128, 512)
    rsw = carve_ap(b.arena, 51200, BF16, 64, 512)
    rt1 = carve_ap(b.arena, 52224, BF16, 64, 512)
    rt2 = carve_ap(b.arena, 53248, BF16, 64, 512)
    rows = carve_ap(b.arena, 54272, F32, 1, 1024)
    blv_all = carve_ap(b.arena, 59392, F32, 1, 2048)
    sv_all = carve_ap(b.arena, 67584, BF16, 1, 2048)
    mxl = carve_ap(b.arena, 58368, F32, 128, 1)
    sml = carve_ap(b.arena, 58372 + 12, F32, 128, 1)
    mxc = carve_ap(b.arena, 58400, F32, 128, 1)
    smc = carve_ap(b.arena, 58416, F32, 128, 1)
    rl_ = carve_ap(b.arena, 58432, F32, 128, 1)
    rc_ = carve_ap(b.arena, 58448, F32, 128, 1)

    pP, pSt, pSL, pSC, pCP, pLV = b.pb[0], b.pb[1], b.pb[2], b.pb[3], b.pb[4], b.pb[5]
    phL, phC = b.pbh[0], b.pbh[1]

    c.memset("dve", lkb[:, 0:128], 0.0, waits=pw, inc="dve")
    c.memset("dve", lvt[:, 0:128], 0.0, inc="dve")
    d_blv = c.dma("sync", blv_all, b.wt["b_lv0r"][0:1, :], waits=pw)
    d_sv = c.dma("pool", sv_all, b.wt["sink_v"][0:1, :], waits=pw)

    slab_use = {}
    row_use = {}
    prev = dict(proj_evac=0, stats_rd=0, bcast_rd=0, sl_rd=0, sc_rd=0, cp_rd=0,
                tpl_rd=0, tpc_rd=0, eln_rd=0, ecn_rd=0, lv_evac=0, ctx_dma=0, exps=0)

    def proj_norm(dst, slab, bias_col, nw_col, tg, do_norm):
        pen0 = None
        for ci in range(NCH):
            c.mm(pP[:, 0:512], slab[:, ci * 128:(ci + 1) * 128],
                 b.xn[:, ci * S + tg * 512: ci * S + tg * 512 + 512],
                 start=(ci == 0), stop=(ci == NCH - 1),
                 waits=(("act", prev["proj_evac"]),) + pw + c.w(("pdma", None)),
                 inc="pe" if ci == NCH - 1 else None)
        pen = c.c("pe")
        prev["proj_evac"] = c.act(qf32, pP[:, 0:512], AF.Identity, bias=bias_col,
                                  waits=(("pe", pen),) + c.w(("dve", None)), inc="act")
        c.act(sqf, qf32, AF.Square, waits=c.w(("pe", prev["stats_rd"]) if False else ("pe", prev["stats_rd"]),), inc="act")
        c.mm(pSt[0:1, 0:512], onesc, sqf, waits=c.w(("act", None), ("dve", prev["bcast_rd"])), inc="pe")
        prev["stats_rd"] = c.c("pe")
        c.act(rows[0:1, 0:512], pSt[0:1, 0:512], AF.Ln, bias=b.constk[0:1, 1:2],
              scale=1.0 / 128, waits=c.w(("pe", None)), inc="act")
        c.act(rows[0:1, 512:1024], rows[0:1, 0:512], AF.Exp, scale=-0.5,
              bias=b.constk[0:1, 2:3] if do_norm == "q" else 0.0, inc="act")
        c.mm(pSt[:, 0:512], onesr, rows[0:1, 512:1024], waits=c.w(("act", None)), inc="pe")
        bn = c.c("pe")
        sl = slice(tg * 512, tg * 512 + 512)
        prev["bcast_rd"] = c.stt("dve", dst[:, sl], qf32, nw_col, pSt[:, 0:512],
                                 ALU.mult, ALU.mult, waits=(("pe", bn),), inc="dve")
        c.cp("dve", rsw[0:32, :], dst[32:64, sl], inc="dve")
        c.cp("dve", rsw[32:64, :], dst[0:32, sl], inc="dve")
        c.tt("dve", rt1, dst[0:64, sl], b.t["l0cos"][:, tg * 512:(tg + 1) * 512], ALU.mult, inc="dve")
        c.tt("dve", rt2, rsw, b.t["l0ss"][:, tg * 512:(tg + 1) * 512], ALU.mult, inc="dve")
        c.tt("dve", dst[0:64, sl], rt1, rt2, ALU.add, inc="dve")

    for h in range(ATT0_HEADS):
        sw = list(pw)
        if h >= 2:
            sw.append(("pe", slab_use[h - 2]))
        dq = c.dma("pool", qsl[h % 2], b.wt["w_q0"][h, :, :], waits=tuple(sw))
        dlk = c.dma("pool", lksl[h % 2], b.wt["w_lk0"][h, :, :], waits=tuple(sw))
        dlv = c.dma("pool", lvsl[h % 2], b.wt["w_lv0"][h, :, :], waits=tuple(sw))

        for tg in range(NTG):
            c.op("pe", lambda e: e.nop(), waits=(("pdma", dq),))
            proj_norm(qh, qsl[h % 2], sm["bq"][:, h:h + 1], sm["qn"][:, 0:1], tg, "q")
        for tg in range(NTG):
            c.op("pe", lambda e: e.nop(), waits=(("pdma", dlk),))
            proj_norm(lkb[:, 128:2176], lksl[h % 2], sm["blk"][:, h:h + 1], sm["kn"][:, 0:1], tg, "k")
        for tt in range(NTT):
            for ci in range(NCH):
                c.mm(pLV[:, 0:128], b.xn[:, ci * S + tt * 128: ci * S + tt * 128 + 128],
                     lvsl[h % 2][:, ci * 128:(ci + 1) * 128],
                     start=(ci == 0), stop=False,
                     waits=(("pdma", dlv), ("sdma", d_blv), ("dve", prev["lv_evac"])))
            c.mm(pLV[:, 0:128], onesr, blv_all[0:1, h * 128:(h + 1) * 128],
                 start=False, stop=True, inc="pe")
            prev["lv_evac"] = c.cp("dve", lvt[:, (tt + 1) * 128:(tt + 2) * 128], pLV[:, 0:128],
                                   waits=c.w(("pe", None)), inc="dve")
        qlk_ready = c.c("dve")
        for qt in range(NTT):
            mask_sel = b.t["mask_loc0"] if qt == 0 else b.t["mask_loc"]
            c.mm(pSL[:, 0:256], qh[:, qt * 128:(qt + 1) * 128],
                 lkb[:, qt * 128: qt * 128 + 256],
                 waits=(("dve", qlk_ready), ("dve", prev["sl_rd"])), inc="pe", nobar=ATT0_NOBAR)
            prev["sl_rd"] = c.tt("dve", scl, pSL[:, 0:256], mask_sel, ALU.add,
                                 waits=c.w(("pe", None)) + (("act", prev["exps"]),), inc="dve")
            c.red("dve", mxl, scl, ALU.max, negate=True, inc="dve")
            c.act(el, scl, AF.Exp, bias=mxl,
                  waits=c.w(("dve", None)), inc="act")
            n1 = c.red("dve", sml, el, ALU.add, waits=c.w(("act", None)), inc="dve")
            n2 = c.recip(rl_, sml, waits=(("dve!", n1),), inc="dve")
            c.op("dve", lambda e: e.engine_nop(), waits=(("dve!", n2),), inc="dve")
            c.ts("dve", eln, el, rl_, ALU.mult, 0.5, ALU.mult,
                 waits=(("pe", prev["eln_rd"]),), inc="dve")
            c.mm(pSC[:, 0:128], qh[:, qt * 128:(qt + 1) * 128], sm["ck"],
                 waits=(("dve", qlk_ready), ("dve", prev["sc_rd"]), ("pdma", d_sv)),
                 inc="pe", nobar=ATT0_NOBAR)
            c.mm(pSC[:, 128:129], qh[:, qt * 128:(qt + 1) * 128], sm["sinkk"][:, h:h + 1],
                 inc="pe", nobar=ATT0_NOBAR)
            prev["sc_rd"] = c.tt("dve", scc[:, 0:128], pSC[:, 0:128],
                                 b.t["cmaskb"][:, 128 - 8 * qt: 256 - 8 * qt], ALU.add,
                                 waits=c.w(("pe", None)) + (("act", prev["exps"]),), inc="dve")
            prev["sc_rd"] = c.cp("dve", scc[:, 128:129], pSC[:, 128:129], inc="dve")
            c.red("dve", mxc, scc[:, 0:129], ALU.max, negate=True, inc="dve")
            c.act(ec[:, 0:129], scc[:, 0:129], AF.Exp, bias=mxc,
                  waits=c.w(("dve", None)), inc="act")
            m1 = c.red("dve", smc, ec[:, 0:129], ALU.add, waits=c.w(("act", None)), inc="dve")
            m2 = c.recip(rc_, smc, waits=(("dve!", m1),), inc="dve")
            c.op("dve", lambda e: e.engine_nop(), waits=(("dve!", m2),), inc="dve")
            c.ts("dve", ecn[:, 0:129], ec[:, 0:129], rc_, ALU.mult, 0.5, ALU.mult,
                 waits=(("pe", prev["ecn_rd"]),), inc="dve")
            prev["exps"] = c.c("act")
            dv_ready = c.c("dve")
            c.tr(phL[:, 0:128], eln[:, 0:128], idb,
                 waits=(("dve", dv_ready), ("dve", prev["tpl_rd"])), nobar=ATT0_NOBAR)
            c.tr(phL[:, 128:256], eln[:, 128:256], idb, inc="pe", nobar=ATT0_NOBAR)
            prev["eln_rd"] = c.c("pe")
            c.tr(phC[:, 0:128], ecn[:, 0:128], idb, waits=(("dve", prev["tpc_rd"]),), nobar=ATT0_NOBAR)
            c.tr(phC[0:1, 128:256], ecn[:, 128:129], idb, inc="pe", nobar=ATT0_NOBAR)
            prev["ecn_rd"] = c.c("pe")
            prev["tpl_rd"] = c.cp("dve", atl, phL[:, 0:256], waits=c.w(("pe", None)), inc="dve")
            c.cp("dve", atc, phC[:, 0:128], inc="dve")
            prev["tpc_rd"] = c.cp("dve", srow, phC[0:1, 128:256], inc="dve")
            tp_ev = c.c("dve")
            c.mm(pCP[:, 0:128], lvt[:, qt * 128:(qt + 1) * 128], atl[:, 0:128],
                 start=True, stop=False, waits=(("dve", tp_ev), ("dve", prev["cp_rd"])), nobar=ATT0_NOBAR)
            c.mm(pCP[:, 0:128], lvt[:, (qt + 1) * 128:(qt + 2) * 128], atl[:, 128:256],
                 start=False, stop=False, nobar=ATT0_NOBAR)
            c.mm(pCP[:, 0:128], sm["cv"], atc, start=False, stop=False, nobar=ATT0_NOBAR)
            c.mm(pCP[:, 0:128], sv_all[0:1, h * 128:(h + 1) * 128], srow,
                 start=False, stop=True, inc="pe", nobar=ATT0_NOBAR)
            prev["cp_rd"] = c.cp("dve", ctxh[:, qt * 128:(qt + 1) * 128], pCP[:, 0:128],
                                 waits=c.w(("pe", None)) + (("sdma", prev["ctx_dma"]),), inc="dve")
        row_use[h] = c.c("pe")
        slab_use[h] = c.c("pe")
        prev["ctx_dma"] = c.dma("sync", b.CTX0[h, :, :], ctxh, waits=c.w(("dve", None)))
    b.att0_done = c.c("dve")
    b.att0_taps = {"qh": qh, "lkb": lkb, "lvt": lvt, "scl": scl, "eln": eln,
                   "atl": atl, "ecn": ecn, "atc": atc, "scc": scc}


def emit_wo(b, layer):
    c = b.c
    pw = phase_barrier(b)
    sm = b.small
    onesc = b.t["ones_col_f"]
    onesr = b.t["ones_row_f"]
    idf = b.t["idf"]
    at = carve_ap
    if layer == 0:
        ctg = at(b.arena, 0, BF16, 128, 8192)
        wosl = [at(b.arena, 16384 + i * 4096, BF16, 128, 2048) for i in range(2)]
        f0g = at(b.arena, 24576, F32, 128, 8192)
        tmp2 = at(b.arena, 57344, F32, 128, 512)
        y2 = [at(b.arena, 59392 + i * 2048, F32, 128, 512) for i in range(2)]
        sq2 = at(b.arena, 63488, F32, 128, 512)
        rows = at(b.arena, 65536, F32, 1, 2048)
        tmp3 = at(b.arena, 73728, F32, 128, 512)
        W, SRC = b.wt["w_o0"], b.CTX0
        pS1, pS2, pM, pR = b.pb[2], b.pb[3], b.pb[4], b.pb[5]
        pwo = [b.pb[0], b.pb[1]]
        slab_use = {}
        evac = {}
        tile_use = {}
        st_cnt = 0
        f0_rd = 0
        for tg in range(NTG):
            dctg = c.dma("sync", ctg.rearrange("p (h t) -> p h t", h=16),
                         SRC[:, :, tg * 512:(tg + 1) * 512].rearrange("h p t -> p h t"),
                         waits=pw + c.w(("pe", None)))
            df0 = c.dma("sync", f0g.rearrange("p (ch t) -> p ch t", ch=NCH),
                        b.F0[:, :, tg * 512:(tg + 1) * 512].rearrange("ch p t -> p ch t"),
                        waits=pw + (b.ffn_done if hasattr(b, "ffn_done") else ()) +
                              (("dve", f0_rd),))
            for m in range(NCH):
                sw = list(pw)
                if m >= 2:
                    sw.append(("pe", slab_use[m - 2]))
                dwo = c.dma("pool", wosl[m % 2], W[m, :, :], waits=tuple(sw))
                for hh in range(NH0):
                    c.mm(pwo[m % 2][:, 0:512], wosl[m % 2][:, hh * 128:(hh + 1) * 128],
                         ctg[:, hh * 512:(hh + 1) * 512],
                         start=(hh == 0), stop=(hh == NH0 - 1),
                         waits=(("pdma", dwo), ("sdma", dctg),
                                ("act", evac.get(("a", m - 2), 0))),
                         inc="pe" if hh == NH0 - 1 else None)
                evac[("a", m)] = c.act(tmp2, pwo[m % 2][:, 0:512], AF.Identity,
                                       bias=sm["bo"][:, m:m + 1],
                                       waits=c.w(("pe", None), ("dve", None)), inc="act")
                evac[m] = c.tt("dve", y2[m % 2], tmp2, f0g[:, m * 512:(m + 1) * 512],
                               ALU.add,
                               waits=c.w(("act", None)) + (("sdma", df0), ("sdma", st_cnt),
                                                          ("pe", evac.get(("s", m - 2), 0))),
                               inc="dve")
                st_cnt = c.dma("sync", b.Y0fm[m, :, tg * 512:(tg + 1) * 512], y2[m % 2],
                               waits=c.w(("dve", None)))
                c.act(sq2, y2[m % 2], AF.Square, waits=c.w(("dve", None), ("pe", None)),
                      inc="act")
                c.mm(pS1[0:1, 0:512], onesc, y2[m % 2], start=(m == 0), stop=(m == NCH - 1),
                     waits=c.w(("dve", None)), inc="pe")
                evac[("s", m)] = c.c("pe")
                c.mm(pS2[0:1, 0:512], onesc, sq2, start=(m == 0), stop=(m == NCH - 1),
                     waits=c.w(("act", None)), inc="pe")
                slab_use[m] = c.c("pe")
            f0_rd = c.c("dve")
            sx = rows[0:1, 0:512]
            sx2 = rows[0:1, 512:1024]
            mslice = rows[0:1, 1024:1536]
            rslice = rows[0:1, 1536:2048]
            c.ts("dve", mslice, pS1[0:1, 0:512], 1.0 / DM, ALU.mult,
                 waits=c.w(("pe", None)), inc="dve")
            c.ts("dve", sx2, pS2[0:1, 0:512], 1.0 / DM, ALU.mult, inc="dve")
            c.tt("dve", sx, mslice, mslice, ALU.mult, inc="dve")
            c.tt("dve", sx2, sx2, sx, ALU.subtract, inc="dve")
            c.act(rslice, sx2, AF.Ln, bias=b.constk[0:1, 0:1], scale=1.0,
                  waits=c.w(("dve", None)), inc="act")
            c.act(rslice, rslice, AF.Exp, scale=-0.5, inc="act")
            c.mm(pM[:, 0:512], onesr, mslice, waits=c.w(("act", None), ("dve", None)), inc="pe")
            c.mm(pR[:, 0:512], onesr, rslice, inc="pe")
            bn = c.c("pe")
            dyt = c.dma("sync", f0g.rearrange("p (ch t) -> p ch t", ch=NCH),
                        b.Y0fm[:, :, tg * 512:(tg + 1) * 512].rearrange("ch p t -> p ch t"),
                        waits=(("sdma", st_cnt), ("dve", f0_rd)))
            for m in range(NCH):
                c.tt("dve", tmp3, f0g[:, m * 512:(m + 1) * 512], pM[:, 0:512], ALU.subtract,
                     waits=(("sdma", dyt), ("pe", bn)), inc="dve")
                tile_use[m] = c.tt("dve", b.xn[:, m * S + tg * 512: m * S + tg * 512 + 512],
                                   tmp3, pR[:, 0:512], ALU.mult, inc="dve")
            f0_rd = c.c("dve")
        b.wo0_done = c.c("dve")
    else:
        ctg4 = at(b.arena, 0, BF16, 128, 2048)
        wosl = [at(b.arena, 4096 + i * 4096, BF16, 128, 2048) for i in range(2)]
        f1g = at(b.arena, 12288, BF16, 128, 8192)
        y0g = at(b.arena, 28672, F32, 128, 8192)
        otm = [at(b.arena, 61440 + i * 8192, F32, 128, 2048) for i in range(2)]
        tmp2 = at(b.arena, 77824, F32, 128, 128)
        y2 = at(b.arena, 78336, F32, 128, 128)
        W, SRC = b.wt["w_o1"], b.O1
        pT = b.pb[5]
        pwo = [b.pb[0], b.pb[1]]
        evac = {}
        st = {}
        ld_rd = 0
        for tg in range(NTG):
            df1 = c.dma("pool", f1g.rearrange("p (ch t) -> p ch t", ch=NCH),
                        b.F1[:, :, tg * 512:(tg + 1) * 512].rearrange("ch p t -> p ch t"),
                        waits=pw + (b.ffn_done if hasattr(b, "ffn_done") else ()) +
                              (("dve", ld_rd),))
            dy0 = c.dma("sync", y0g.rearrange("p (ch t) -> p ch t", ch=NCH),
                        b.Y0fm[:, :, tg * 512:(tg + 1) * 512].rearrange("ch p t -> p ch t"),
                        waits=pw + (("dve", ld_rd),))
            for t4 in range(4):
                tti = tg * 4 + t4
                dctg = c.dma("sync", ctg4.rearrange("p (h t) -> p h t", h=16),
                             SRC[:, :, tti * 128:(tti + 1) * 128].rearrange("h p t -> p h t"),
                             waits=pw + c.w(("pe", None)))
                for m in range(NCH):
                    li = t4 * NCH + m
                    j = m % 2
                    sw = list(pw)
                    if li >= 2:
                        sw.append(("pe", st.get(("wu", li - 2), 0)))
                    dwo = c.dma("pool", wosl[j], W[m, :, :], waits=tuple(sw))
                    for hh in range(NCH):
                        c.mm(pwo[j][:, 0:128], wosl[j][:, hh * 128:(hh + 1) * 128],
                             ctg4[:, hh * 128:(hh + 1) * 128],
                             start=(hh == 0), stop=(hh == NCH - 1),
                             waits=(("pdma", dwo), ("sdma", dctg),
                                    ("dve", st.get(("e", li - 2), 0))),
                             inc="pe" if hh == NCH - 1 else None)
                    st[("e", li)] = c.cp("dve", tmp2, pwo[j][:, 0:128],
                                         waits=c.w(("pe", None)), inc="dve")
                    c.tt("dve", tmp2, tmp2, f1g[:, m * 512 + t4 * 128: m * 512 + t4 * 128 + 128],
                         ALU.add, waits=(("pdma", df1),), inc="dve")
                    st[(li,)] = c.tt("dve", y2, tmp2,
                                     y0g[:, m * 512 + t4 * 128: m * 512 + t4 * 128 + 128],
                                     ALU.add,
                                     waits=(("sdma", dy0), ("pe", st.get(("tr", li - 1), 0))),
                                     inc="dve")
                    st[("tr", li)] = c.tr(pT[:, 0:128], y2, idf,
                                          waits=c.w(("dve", None)) +
                                                (("dve", st.get(("t", li - 1), 0)),),
                                          inc="pe")
                    st[("t", li)] = c.cp("dve", otm[t4 % 2][:, m * 128:(m + 1) * 128],
                                         pT[:, 0:128],
                                         waits=c.w(("pe", None)) +
                                               (("sdma", st.get(("o", t4 - 2), 0)),),
                                         inc="dve")
                    st[("wu", li)] = c.c("pe")
                st[("o", t4)] = c.dma("sync", b.out[tti * 128:(tti + 1) * 128, :], otm[t4 % 2],
                                      waits=c.w(("dve", None)))
            ld_rd = c.c("dve")
            st[("o", -2)] = st.get(("o", 2), 0)
            st[("o", -1)] = st.get(("o", 3), 0)
        b.out_stored = st[("o", 3)]

def emit_gk(b):
    c = b.c
    pw = phase_barrier(b)
    idf = b.t["idf"]
    sgk = carve_ap(b.wta, 0, BF16, 128, 512)
    sig = carve_ap(b.arena, 0, F32, 128, 512)
    glog = carve_ap(b.arena, 2048, F32, 128, 512)
    gr = carve_ap(b.arena, 4096, F32, 128, 32)
    GrT = carve_ap(b.arena, 8192, F32, 32, 2048)
    b.qdT = carve_ap(b.arena, 71680, BF16, 32, 2048)
    b.kdT = carve_ap(b.arena, 75776, BF16, 32, 2048)
    b.dcols = carve_ap(b.arena, 79872, F32, 64, 512)
    dgk = c.dma("pool", sgk, b.wt["w_gk1"][:, :], waits=pw)
    pG = b.pb[0]
    ev = 0
    for tt in range(NTT):
        for ci in range(NCH):
            c.mm(pG[:, 0:32], b.xn[:, ci * S + tt * 128: ci * S + tt * 128 + 128],
                 sgk[:, ci * 32:(ci + 1) * 32], start=(ci == 0), stop=(ci == NCH - 1),
                 waits=(("pdma", dgk), ("act", ev)) + pw, inc="pe" if ci == NCH - 1 else None)
        ev = c.act(sig[:, tt * 32:(tt + 1) * 32], pG[:, 0:32], AF.Sigmoid,
                   waits=c.w(("pe", None)), inc="act")
    c.act(glog, sig, AF.Ln, waits=c.w(("act", None)), inc="act")
    lnn = c.c("act")
    pC, pT2 = b.pb[1], b.pb[2]
    ev2 = 0
    ev3 = 0
    for tt in range(NTT):
        c.mm(pC[:, 0:32], b.t["trif"], glog[:, tt * 32:(tt + 1) * 32],
             waits=(("act", lnn), ("dve", ev2)), inc="pe")
        ev2 = c.cp("dve", gr, pC[:, 0:32], waits=c.w(("pe", None)) + (("pe", ev3),), inc="dve")
        c.tr(pT2[0:32, 0:128], gr, idf, waits=(("dve", ev2), ("dve", ev3 and 0 or 0)), inc="pe")
        ev3 = c.c("pe")
        ev2b = c.cp("dve", GrT[:, tt * 128:(tt + 1) * 128], pT2[0:32, 0:128],
                    waits=c.w(("pe", None)), inc="dve")
    grn = c.c("dve")
    c.act(b.qdT, GrT, AF.Exp, scale=1.0 / 16, waits=(("dve", grn),), inc="act")
    c.act(b.kdT, GrT, AF.Exp, scale=-1.0 / 16, bias=b.constk[0:32, 3:4], inc="act")
    expn = c.c("act")
    pD = b.pb[3]
    qd3 = b.qdT.rearrange("p (t k) -> p t k", k=128)
    dv4 = 0
    for g in range(NCH):
        for hh in range(2):
            c.mm(pD[0:64, 0:16], b.t["selab"][:, (2 * g + hh) * 64:(2 * g + hh + 1) * 64],
                 qd3[:, :, 127], waits=(("act", expn), ("dve", dv4)), inc="pe")
            dv4 = c.cp("dve", b.dcols[:, g * 32 + hh * 16: g * 32 + hh * 16 + 16],
                       pD[0:64, 0:16], waits=c.w(("pe", None)), inc="dve")
    b.gk_done = c.c("dve")


ATT1_GROUPS = NCH
ATT1_NO_CROSS = False
ATT1_PREFETCH = True


def emit_att1(b):
    c = b.c
    pw = phase_barrier(b)
    onesc = b.t["ones_col_f"]
    onesr = b.t["ones_row_f"]
    idb = b.t["idb"]
    at = carve_ap
    og = at(b.arena, 0, BF16, 128, 2048)
    qA = at(b.arena, 4096, BF16, 64, 2048)
    qB = at(b.arena, 8192, BF16, 64, 2048)
    kA = at(b.arena, 12288, BF16, 64, 2048)
    kB = at(b.arena, 16384, BF16, 64, 2048)
    sgA = at(b.arena, 20480, BF16, 64, 2048)
    sgB = at(b.arena, 24576, BF16, 64, 2048)
    vtm = at(b.arena, 28672, BF16, 128, 2048)
    sl_q = [at(b.arena, 32768 + i * 4096, BF16, 128, 2048) for i in range(2)]
    sl_k = [at(b.arena, 40960 + i * 4096, BF16, 128, 2048) for i in range(2)]
    sl_g = [at(b.arena, 49152 + i * 4096, BF16, 128, 2048) for i in range(2)]
    sl_v = [at(b.arena, 57344 + i * 4096, BF16, 128, 2048) for i in range(2)]
    tmpg = at(b.arena, 65536, BF16, 64, 512)
    q2A = at(b.arena, 65536, BF16, 64, 128)
    q2B = at(b.arena, 65792, BF16, 64, 128)
    k2A = at(b.arena, 66048, BF16, 64, 128)
    k2B = at(b.arena, 66304, BF16, 64, 128)
    k2AT = at(b.arena, 66560, BF16, 128, 64)
    k2BT = at(b.arena, 66688, BF16, 128, 64)
    aTA = at(b.arena, 66816, BF16, 128, 128)
    aTB = at(b.arena, 67072, BF16, 128, 128)
    MA = at(b.arena, 67328, BF16, 64, 64)
    MB = at(b.arena, 67456, BF16, 64, 64)
    mdA = at(b.arena, 67584, F32, 64, 64)
    mdB = at(b.arena, 67840, F32, 64, 64)
    poA = at(b.arena, 68096, F32, 64, 128)
    poB = at(b.arena, 68608, F32, 64, 128)
    oA = at(b.arena, 69120, F32, 64, 128)
    oB = at(b.arena, 69632, F32, 64, 128)
    sqo = at(b.arena, 70144, F32, 64, 128)
    vrA = at(b.arena, 70656, F32, 1, 128)
    vrB = at(b.arena, 71168, F32, 1, 128)
    shufb = at(b.arena, 68096, BF16, 64, 512)
    rt1 = at(b.arena, 69120, BF16, 64, 512)
    rt2 = at(b.arena, 70144, BF16, 64, 512)

    pS_A, pS_B, pPO, pU, pD4, pV = b.pb[0], b.pb[1], b.pb[2], b.pb[3], b.pb[4], b.pb[5]
    phT = b.pbh[0]
    XOR1 = [i ^ 1 for i in range(32)]

    slab_use = {}
    prev = dict(pe_proj=0, og_dma=0, v_ev=0, sA=0, sB=0, po_ev=0, u_rd=0, d_rd=0,
                q2free=0, po_cp=0, ktn=0, var_rd=0, m_up_A=0, m_up_B=0)

    for g in range(ATT1_GROUPS):
        sw = list(pw)
        if g >= 2:
            sw.append(("pe", slab_use[g - 2]))
        d_q = c.dma("pool", sl_q[g % 2], b.wt["w_q1"][g, :, :], waits=tuple(sw))
        d_k = c.dma("pool", sl_k[g % 2], b.wt["w_k1"][g, :, :], waits=tuple(sw))
        d_g = c.dma("pool", sl_g[g % 2], b.wt["w_g1"][g, :, :], waits=tuple(sw))
        d_v = c.dma("pool", sl_v[g % 2], b.wt["w_v1"][g, :, :], waits=tuple(sw))
        evp = 0
        for which, slab, dd in (("q", sl_q[g % 2], d_q), ("k", sl_k[g % 2], d_k),
                                ("gg", sl_g[g % 2], d_g)):
            for tg in range(NTG):
                for ci in range(NCH):
                    c.mm(pS_A[:, 0:512], slab[:, ci * 128:(ci + 1) * 128],
                         b.xn[:, ci * S + tg * 512: ci * S + tg * 512 + 512],
                         start=(ci == 0), stop=(ci == NCH - 1),
                         waits=(("pdma", dd), ("dve", prev["sA"]), ("act", evp)) + pw,
                         inc="pe" if ci == NCH - 1 else None)
                sl2 = slice(tg * 512, tg * 512 + 512)
                if which == "q":
                    c.cp("dve", qA[:, sl2], pS_A[0:64, 0:512], waits=c.w(("pe", None)), inc="dve")
                    prev["sA"] = c.cp("dve", qB[:, sl2], pS_A[64:128, 0:512], inc="dve")
                elif which == "k":
                    c.cp("dve", kA[:, sl2], pS_A[0:64, 0:512], waits=c.w(("pe", None)), inc="dve")
                    prev["sA"] = c.cp("dve", kB[:, sl2], pS_A[64:128, 0:512], inc="dve")
                else:
                    c.act(sgA[:, sl2], pS_A[0:64, 0:512], AF.Silu,
                          waits=c.w(("pe", None)), inc="act")
                    gcp = c.cp("dve", tmpg, pS_A[64:128, 0:512],
                               waits=c.w(("pe", None), ("act", None)), inc="dve")
                    evp = c.act(sgB[:, sl2], tmpg, AF.Silu, waits=(("dve", gcp),), inc="act")
                    prev["sA"] = c.c("dve")
        for tt in range(NTT):
            for ci in range(NCH):
                c.mm(pV[:, 0:128], b.xn[:, ci * S + tt * 128: ci * S + tt * 128 + 128],
                     sl_v[g % 2][:, ci * 128:(ci + 1) * 128],
                     start=(ci == 0), stop=(ci == NCH - 1),
                     waits=(("pdma", d_v), ("dve", prev["v_ev"])),
                     inc="pe" if ci == NCH - 1 else None)
            prev["v_ev"] = c.cp("dve", vtm[:, tt * 128:(tt + 1) * 128], pV[:, 0:128],
                                waits=c.w(("pe", None)), inc="dve")
        for buf in (qA, qB, kA, kB):
            for tg in range(NTG):
                sl2 = slice(tg * 512, tg * 512 + 512)
                c.shuf(shufb, buf[:, sl2], XOR1, waits=c.w(("pe", None), ("act", None)), inc="dve")
                c.tt("dve", rt1, buf[:, sl2], b.t["l1cos"][0:64, tg * 512:(tg + 1) * 512],
                     ALU.mult, inc="dve")
                c.tt("dve", rt2, shufb, b.t["l1ss"][0:64, tg * 512:(tg + 1) * 512],
                     ALU.mult, inc="dve")
                c.tt("dve", buf[:, sl2], rt1, rt2, ALU.add, inc="dve")
        rope_done = c.c("dve")
        for t in range(NTT):
            tsl = slice(t * 128, (t + 1) * 128)
            c.mm(pD4[0:64, 0:128], b.t["selab"][:, (2 * g) * 64:(2 * g + 1) * 64],
                 b.qdT[:, tsl], waits=(("dve", prev["d_rd"]), ("dve", rope_done),
                                       ("dve", prev["var_rd"])),
                 inc="pe")
            c.mm(pD4[0:64, 128:256], b.t["selab"][:, (2 * g + 1) * 64:(2 * g + 2) * 64],
                 b.qdT[:, tsl], inc="pe")
            c.mm(pD4[0:64, 256:384], b.t["selab"][:, (2 * g) * 64:(2 * g + 1) * 64],
                 b.kdT[:, tsl], inc="pe")
            c.mm(pD4[0:64, 384:512], b.t["selab"][:, (2 * g + 1) * 64:(2 * g + 2) * 64],
                 b.kdT[:, tsl], inc="pe")
            dn = c.c("pe")
            n1 = c.tt("dve", q2A, qA[:, tsl], pD4[0:64, 0:128], ALU.mult,
                      waits=(("pe", dn), ("pe", prev["q2free"])), inc="dve")
            c.tt("dve", q2B, qB[:, tsl], pD4[0:64, 128:256], ALU.mult, inc="dve")
            c.tt("dve", k2A, kA[:, tsl], pD4[0:64, 256:384], ALU.mult, inc="dve")
            prev["d_rd"] = c.tt("dve", k2B, kB[:, tsl], pD4[0:64, 384:512], ALU.mult, inc="dve")
            qk_n = c.c("dve")
            c.mm(pS_A[:, 0:128], k2A, q2A, waits=(("dve", qk_n), ("dve", prev["sA"])), inc="pe")
            c.mm(pS_B[:, 0:128], k2B, q2B, waits=(("dve", prev["sB"]),), inc="pe")
            sn = c.c("pe")
            prev["sA"] = c.tt("dve", aTA, pS_A[:, 0:128], b.t["l1maskb"], ALU.mult,
                              waits=(("pe", sn),), inc="dve")
            prev["sB"] = c.tt("dve", aTB, pS_B[:, 0:128], b.t["l1maskb"], ALU.mult, inc="dve")
            a_n = c.c("dve")
            c.tr(phT[:, 0:64], k2A, idb[0:64, 0:64], waits=(("dve", qk_n), ("dve", prev["ktn"])))
            c.tr(phT[:, 64:128], k2B, idb[0:64, 0:64], inc="pe")
            tn = c.c("pe")
            c.cp("dve", k2AT, phT[:, 0:64], waits=(("pe", tn),), inc="dve")
            ktn = c.cp("dve", k2BT, phT[:, 64:128], inc="dve")
            prev["ktn"] = ktn
            stopA = (t == 0) or ATT1_NO_CROSS
            c.mm(pPO[0:64, 0:128], vtm[:, t * 128: t * 128 + 64], aTA,
                 start=True, stop=stopA, waits=(("dve", a_n), ("act", prev["po_ev"]),
                                                ("dve", prev["po_cp"])), inc="pe")
            if t > 0 and not ATT1_NO_CROSS:
                c.mm(pPO[0:64, 0:128], MA, q2A, start=False, stop=True,
                     waits=(("dve", prev["m_up_A"]),), inc="pe")
            c.mm(pPO[0:64, 128:256], vtm[:, t * 128 + 64: t * 128 + 128], aTB,
                 start=True, stop=stopA, inc="pe")
            if t > 0 and not ATT1_NO_CROSS:
                c.mm(pPO[0:64, 128:256], MB, q2B, start=False, stop=True,
                     waits=(("dve", prev["m_up_B"]),), inc="pe")
            po_n = c.c("pe")
            c.mm(pU[0:64, 0:64], k2AT, vtm[:, t * 128: t * 128 + 64],
                 waits=(("dve", ktn), ("dve", prev["u_rd"])), inc="pe")
            c.mm(pU[0:64, 64:128], k2BT, vtm[:, t * 128 + 64: t * 128 + 128], inc="pe")
            un = c.c("pe")
            dA = b.dcols[:, g * 32 + t: g * 32 + t + 1]
            dB = b.dcols[:, g * 32 + 16 + t: g * 32 + 16 + t + 1]
            if t == 0:
                m1 = c.ts("dve", MA, pU[0:64, 0:64], dA, ALU.mult, waits=(("pe", un),), inc="dve")
                m2 = c.ts("dve", MB, pU[0:64, 64:128], dB, ALU.mult, inc="dve")
                prev["m_up_A"] = m1
                prev["m_up_B"] = m2
                prev["u_rd"] = m2
            else:
                w1 = c.ts("dve", mdA, MA, dA, ALU.mult, waits=(("pe", un), ("pe", po_n)), inc="dve")
                w2 = c.ts("dve", mdB, MB, dB, ALU.mult, inc="dve")
                m1 = c.stt("dve", MA, pU[0:64, 0:64], dA, mdA, ALU.mult, ALU.add,
                           waits=(("dve!", w1),), inc="dve")
                m2 = c.stt("dve", MB, pU[0:64, 64:128], dB, mdB, ALU.mult, ALU.add,
                           waits=(("dve!", w2),), inc="dve")
                prev["m_up_A"] = m1
                prev["m_up_B"] = m2
                prev["u_rd"] = m2
            c.act(sqo, pPO[0:64, 0:128], AF.Square, waits=(("pe", po_n),), inc="act")
            c.mm(pD4[0:1, 0:128], onesc[0:64, 0:1], sqo, waits=c.w(("act", None), ("dve", qk_n)), inc="pe")
            c.act(sqo, pPO[0:64, 128:256], AF.Square, waits=c.w(("pe", None)), inc="act")
            c.mm(pD4[0:1, 128:256], onesc[0:64, 0:1], sqo, waits=c.w(("act", None)), inc="pe")
            c.act(vrA, pD4[0:1, 0:128], AF.Ln, bias=b.constk[0:1, 1:2], scale=1.0 / 64,
                  waits=c.w(("pe", None)), inc="act")
            c.act(vrA, vrA, AF.Exp, scale=-0.5, inc="act")
            c.act(vrB, pD4[0:1, 128:256], AF.Ln, bias=b.constk[0:1, 1:2], scale=1.0 / 64, inc="act")
            c.act(vrB, vrB, AF.Exp, scale=-0.5, inc="act")
            c.mm(pD4[0:64, 256:384], onesr[0:1, 0:64], vrA, waits=c.w(("act", None)), inc="pe")
            c.mm(pD4[0:64, 384:512], onesr[0:1, 0:64], vrB, inc="pe")
            bn = c.c("pe")
            c.cp("dve", poA, pPO[0:64, 0:128], waits=(("pe", bn),), inc="dve")
            pon = c.cp("dve", poB, pPO[0:64, 128:256], inc="dve")
            prev["po_cp"] = pon
            prev["po_ev"] = c.c("act")
            c.tt("dve", oA, poA, pD4[0:64, 256:384], ALU.mult, waits=(("dve!", pon),), inc="dve")
            c.tt("dve", oB, poB, pD4[0:64, 384:512], ALU.mult, inc="dve")
            prev["var_rd"] = c.c("dve")
            c.tt("dve", og[0:64, tsl], oA, sgA[:, tsl], ALU.mult,
                 waits=(("sdma", prev["og_dma"]),), inc="dve")
            c.tt("dve", og[64:128, tsl], oB, sgB[:, tsl], ALU.mult, inc="dve")
            prev["q2free"] = un
            if g == 0 and t == 15:
                snap6 = carve_ap(b.wta, 3840, BF16, 64, 128)
                c.cp("dve", snap6, og[0:64, 128:256], inc="dve")
                b.att1_snap6 = snap6
            if g == 0 and t == 1:
                snap = carve_ap(b.wta, 2048, BF16, 128, 128)
                snap2 = carve_ap(b.wta, 2304, BF16, 64, 128)
                snap3 = carve_ap(b.wta, 2560, F32, 64, 128)
                snap4 = carve_ap(b.wta, 3072, F32, 64, 128)
                snap5 = carve_ap(b.wta, 3584, BF16, 64, 128)
                c.cp("dve", snap, aTA, inc="dve")
                c.cp("dve", snap2, q2A, inc="dve")
                c.cp("dve", snap3, poA, inc="dve")
                c.cp("dve", snap4, oA, inc="dve")
                c.cp("dve", snap5, og[0:64, tsl], inc="dve")
                b.att1_snaps = {"snA": snap, "snQ2": snap2, "snPO": snap3,
                                "snO": snap4, "snOG": snap5}
        slab_use[g] = c.c("pe")
        prev["og_dma"] = c.dma("sync", b.O1[g, :, :], og, waits=c.w(("dve", None)))
    b.att1_done = c.w(("sdma", None))
    b.att1_taps = {"a1qA": qA, "a1qB": qB, "a1kA": kA, "a1vtm": vtm, "a1sgA": sgA,
                   "a1qdT": b.qdT, "a1kdT": b.kdT, "a1dcols": b.dcols, "a1og": og, "a1sgB": sgB,
                   "a1q2A": q2A, "a1k2A": k2A, "a1aTA": aTA}


def emit_debug(b):
    c = b.c
    if not b.debug:
        return
    allw = c.w(("dve", None), ("pe", None), ("act", None), ("pool", None),
               ("sdma", None))
    nc = b.nc

    def dump(name, shape, src_ap, queue="pool"):
        if name not in b.debug:
            return
        t = nc.dram_tensor("dbg_" + name, list(shape), F32, kind="ExternalOutput").ap()
        b.dbg_tensors[name] = t
        idx = tuple([slice(None)] * len(shape))
        c.dma(queue, t[idx], src_ap, waits=allw)

    dump("xn", [128, NCH * S], b.xn)
    dump("entr", [128, 2048], b.small["entr"])
    dump("ck", [128, 128], b.small["ck"])
    dump("cv", [128, 128], b.small["cv"])
    if "snOG2" in b.debug and hasattr(b, "att1_snap6"):
        ap = b.att1_snap6
        t = nc.dram_tensor("dbg_snOG2", [64, 128], F32, kind="ExternalOutput").ap()
        b.dbg_tensors["snOG2"] = t
        c.dma("pool", t[:, :], ap, waits=allw)
    for nm in ("snA", "snQ2", "snPO", "snO", "snOG"):
        if nm in b.debug and hasattr(b, "att1_snaps"):
            ap = b.att1_snaps[nm]
            t = nc.dram_tensor("dbg_" + nm, [ap.shape[0], ap.shape[1]], F32, kind="ExternalOutput").ap()
            b.dbg_tensors[nm] = t
            c.dma("pool", t[:, :], ap, waits=allw)
    for nm in ("a1qA", "a1qB", "a1kA", "a1vtm", "a1sgA", "a1qdT", "a1kdT", "a1dcols", "a1og", "a1sgB",
               "a1q2A", "a1k2A", "a1aTA"):
        if nm in b.debug and hasattr(b, "att1_taps"):
            ap = b.att1_taps[nm]
            t = nc.dram_tensor("dbg_" + nm, [ap.shape[0], ap.shape[1]], F32, kind="ExternalOutput").ap()
            b.dbg_tensors[nm] = t
            c.dma("pool", t[:, :], ap, waits=allw)
    for nm in ("qh", "lkb", "lvt", "scl", "eln", "atl", "ecn", "atc", "scc"):
        if nm in b.debug and hasattr(b, "att0_taps"):
            ap = b.att0_taps[nm]
            shp = [ap.shape[0], ap.shape[1]]
            t = nc.dram_tensor("dbg_" + nm, shp, F32, kind="ExternalOutput").ap()
            b.dbg_tensors[nm] = t
            c.dma("pool", t[:, :], ap, waits=allw)
    if "o1" in b.debug:
        t = nc.dram_tensor("dbg_o1", [NCH, 128, S], F32, kind="ExternalOutput").ap()
        b.dbg_tensors["o1"] = t
        c.dma("pool", t[:, :, :], b.O1[:, :, :], waits=allw)
    if "y0" in b.debug:
        t = nc.dram_tensor("dbg_y0", [NCH, 128, S], F32, kind="ExternalOutput").ap()
        b.dbg_tensors["y0"] = t
        c.dma("sync", t[:, :, :], b.Y0fm[:, :, :], waits=allw)
    if "ctx0" in b.debug:
        t = nc.dram_tensor("dbg_ctx0", [NH0, 128, S], F32, kind="ExternalOutput").ap()
        b.dbg_tensors["ctx0"] = t
        c.dma("pool", t[:, :, :], b.CTX0[:, :, :], waits=allw)
    for nm, tsr in [("f0", "F0"), ("f1", "F1")]:
        if nm in b.debug:
            t = nc.dram_tensor("dbg_" + nm, [NCH, 128, S], F32, kind="ExternalOutput").ap()
            b.dbg_tensors[nm] = t
            c.dma("sync", t[:, :, :], getattr(b, tsr)[:, :, :], waits=allw)
    if "x0fm" in b.debug:
        t = nc.dram_tensor("dbg_x0fm", [NCH, 128, S], F32, kind="ExternalOutput").ap()
        b.dbg_tensors["x0fm"] = t
        c.dma("sync", t[:, :, :], b.X0fm[:, :, :], waits=allw)


def finalize(b):
    c = b.c
    c.op("sync", lambda e: e.nop(),
         waits=c.w(("sdma", None), ("pdma", None), ("pe", None), ("dve", None),
                   ("act", None), ("pool", None)))



ALL_PHASES = ("ln0", "cmp", "ffn0", "att0", "wo0", "gk", "ffn1", "att1", "wo1")


def run_built(nc, b, inputs, cores=8):
    g = prep_inputs(inputs)
    consts = b.hconsts
    x = np.ascontiguousarray(inputs["L0_attn_hidden_states"].astype(np.float32, copy=False))
    in_maps = []
    for core in range(cores):
        m = {"x": x[core]}
        m.update(g)
        m.update({"c_" + k: v for k, v in consts.items()})
        in_maps.append(m)
    return run_bass_kernel_spmd(nc, in_maps, core_ids=list(range(cores)))


def kernel(**inputs):
    nc, b = build_program(ALL_PHASES)
    res = run_built(nc, b, inputs)
    return np.stack([np.asarray(r["OUT"], np.float32) for r in res.results], 0)
